# version 29
# baseline (speedup 1.0000x reference)
import numpy as np
import ml_dtypes
import concourse.bass as bass
import concourse.mybir as mybir
from concourse.bass_utils import run_bass_kernel_spmd

F32 = mybir.dt.float32
BF16 = mybir.dt.bfloat16
AF = mybir.ActivationFunctionType
OP = mybir.AluOpType
AX = mybir.AxisListType.X

NCORES = 8
TC = 1024
KC = 16
DM = 2048
HC_E = 11
EPS_LN = 1e-5
EPS_DEN = 1e-6
NSLOT = 5
DEBUG = False


class Sem:
    def __init__(self, handle):
        self.h = handle
        self.n = 0


class Sched:

    KEYS = ("t", "v", "s", "g", "y")

    def __init__(self, engine_sems):
        self.q = {k: [] for k in self.KEYS}
        self.esem = engine_sems

    def emit(self, key, fn, waits=(), inc=None, inc_amt=1, dma=False):
        waits = list(waits)
        es = self.esem.get(key)
        if es is not None and not dma:
            waits.append((es, es.n))
            assert inc is None or inc is es
            inc, inc_amt = es, 1
        waits = tuple((s.h, v) for (s, v) in waits if v > 0)
        if inc is not None:
            inc.n += inc_amt
        self.q[key].append((waits, fn, (inc.h if inc is not None else None),
                            inc_amt))

    def run_key(self, key):
        def body(eng):
            if key == "t":
                eng = _SkipGroupCheck(eng)
            for waits, fn, inc_h, amt in self.q[key]:
                for h, v in waits:
                    eng.wait_ge(h, v)
                inst = fn(eng)
                if inc_h is not None:
                    inst.then_inc(inc_h, amt)
        return body


class _SkipGroupCheck:

    def __init__(self, t):
        self._t = t

    def __getattr__(self, k):
        return getattr(self._t, k)

    def matmul(self, *a, **kw):
        kw.setdefault("skip_group_check", True)
        return self._t.matmul(*a, **kw)


def build_program():
    nc = bass.Bass("TRN2", target_bir_lowering=False)

    x_in = nc.dram_tensor("x", [TC, DM], F32, kind="ExternalInput").ap()
    qkv_t = nc.dram_tensor("qkv_t", [48, 128, 2048], F32,
                           kind="ExternalInput").ap()
    wout_t = nc.dram_tensor("wout_t", [16, 128, 2048], F32,
                            kind="ExternalInput").ap()
    gate_t = nc.dram_tensor("gate_t", [128, 128], F32,
                            kind="ExternalInput").ap()
    wg_t = nc.dram_tensor("wg_t", [8, 11, 128, 2048], F32,
                          kind="ExternalInput").ap()
    wu_t = nc.dram_tensor("wu_t", [8, 11, 128, 2048], F32,
                          kind="ExternalInput").ap()
    wd_t = nc.dram_tensor("wd_t", [8, 16, 128, 1408], F32,
                          kind="ExternalInput").ap()
    swg_t = nc.dram_tensor("swg_t", [22, 128, 2048], F32,
                           kind="ExternalInput").ap()
    swu_t = nc.dram_tensor("swu_t", [22, 128, 2048], F32,
                           kind="ExternalInput").ap()
    swd_t = nc.dram_tensor("swd_t", [2, 16, 128, 1408], F32,
                           kind="ExternalInput").ap()
    decay = nc.dram_tensor("decay", [16], F32, kind="ExternalInput").ap()
    percore = nc.dram_tensor("percore", [1, 16], F32, kind="ExternalInput").ap()
    ones_c = nc.dram_tensor("ones_c", [128, 128], F32, kind="ExternalInput").ap()
    ident_c = nc.dram_tensor("ident_c", [128, 128], F32, kind="ExternalInput").ap()
    sele_c = nc.dram_tensor("sele_c", [8, 1024], BF16, kind="ExternalInput").ap()
    out_t = nc.dram_tensor("out", [TC, DM], F32, kind="ExternalOutput").ap()

    if DEBUG:
        dbg_xhat1 = nc.dram_tensor("dbg_xhat1", [128, KC * TC], BF16,
                                   kind="ExternalOutput").ap()
        dbg_outpre = nc.dram_tensor("dbg_outpre", [128, KC * TC], BF16,
                                    kind="ExternalOutput").ap()
        dbg_xres = nc.dram_tensor("dbg_xres", [128, KC * TC], F32,
                                  kind="ExternalOutput").ap()
        dbg_small = nc.dram_tensor("dbg_small", [128, 512], F32,
                                   kind="ExternalOutput").ap()
        dbg_den = nc.dram_tensor("dbg_den", [KC, 2048], F32,
                                 kind="ExternalOutput").ap()

    sp_q = nc.dram_tensor("sp_q", [DM, TC], F32).ap()
    sp_ykv = nc.dram_tensor("sp_ykv", [DM, TC], F32).ap()
    sp_yk = nc.dram_tensor("sp_yk", [DM, TC], F32).ap()
    cc_inA = nc.dram_tensor("cc_inA", [128, 16], F32).ap()
    cc_outA = nc.dram_tensor("cc_outA", [128 * NCORES, 16], F32).ap()
    cc_inB = nc.dram_tensor("cc_inB", [128, 16], F32).ap()
    cc_outB = nc.dram_tensor("cc_outB", [128 * NCORES, 16], F32).ap()
    iota_c = nc.dram_tensor("iota_c", [1, 1024], F32, kind="ExternalInput").ap()

    from contextlib import ExitStack
    with ExitStack() as stack:
        ec = stack.enter_context
        xres = ec(nc.sbuf_tensor("xres", [128, KC * TC], F32))
        xhat = ec(nc.sbuf_tensor("xhat", [128, KC * TC], BF16))
        wbuf = ec(nc.sbuf_tensor("wbuf", [128, NSLOT * 2048], BF16))
        arF = ec(nc.sbuf_tensor("arenaF", [128, 9216], F32))
        gar = ec(nc.sbuf_tensor("garena", [128, HC_E * TC], BF16))
        combbc = ec(nc.sbuf_tensor("combbc", [128, 2 * TC], BF16))
        gwslot = ec(nc.sbuf_tensor("gwslot", [128, 128], BF16))
        onesb = ec(nc.sbuf_tensor("onesb", [128, 128], F32))
        identb = ec(nc.sbuf_tensor("identb", [128, 128], F32))
        seleb = ec(nc.sbuf_tensor("seleb", [8, 1024], BF16))
        combT = ec(nc.sbuf_tensor("combT", [8, 1024], BF16))
        dcol = ec(nc.sbuf_tensor("dcol", [128, 16], F32))
        sbar = ec(nc.sbuf_tensor("sbar", [128, 32], F32))
        gath = ec(nc.sbuf_tensor("gath", [128, 256], F32))
        sin = ec(nc.sbuf_tensor("sin", [128, 32], F32))
        scr32 = ec(nc.sbuf_tensor("scr32", [128, 32], F32))
        scr8 = ec(nc.sbuf_tensor("scr8", [128, 128], F32))
        srow = ec(nc.sbuf_tensor("srow", [1, 160], F32))
        iotarow = ec(nc.sbuf_tensor("iotarow", [1, 1024], F32))
        dprow = ec(nc.sbuf_tensor("dprow", [1, 1024], BF16))
        prow2 = ec(nc.sbuf_tensor("prow2", [1, 2048], F32))
        rbf = ec(nc.sbuf_tensor("rbf", [1, 2048], BF16))
        onesbf = ec(nc.sbuf_tensor("onesbf", [128, 128], BF16))
        pb = [ec(nc.psum_tensor("pb%d" % i, [128, 512], F32))
              for i in range(8)]
        sT = Sem(ec(nc.semaphore("sT")))
        sV = Sem(ec(nc.semaphore("sV")))
        sA = Sem(ec(nc.semaphore("sA")))
        sP = Sem(ec(nc.semaphore("sP")))
        sWs = [Sem(ec(nc.semaphore("sW%d" % i))) for i in range(NSLOT)]
        sXin = [Sem(ec(nc.semaphore("sXin%d" % i))) for i in range(2)]
        sSp = Sem(ec(nc.semaphore("sSp")))
        sIn = [Sem(ec(nc.semaphore("sIn%d" % i))) for i in range(3)]
        sOd = [Sem(ec(nc.semaphore("sOd%d" % i))) for i in range(2)]
        sMisc = Sem(ec(nc.semaphore("sMisc")))
        sCC = Sem(ec(nc.semaphore("sCC")))
        sDbg = Sem(ec(nc.semaphore("sDbg"))) if DEBUG else None
        block = ec(nc.Block())
        print("[kernel build] sbuf bytes remaining:", nc.sbuf_bytes_remaining)

        sch = Sched({"v": sV, "s": sA, "g": sP})
        E = sch.emit

        def fs(lo, hi):
            return arF[:, lo * 1024:hi * 1024]
        S0, S1, S2, S3 = fs(0, 1), fs(1, 2), fs(2, 3), fs(3, 4)
        S4, S5, S6 = fs(4, 5), fs(5, 6), fs(6, 7)
        S7, S8 = fs(7, 8), fs(8, 9)
        mrow = arF[0:1, 0:1024]
        varrow = arF[0:1, 1024:2048]
        arow = arF[0:1, 2048:3072]
        brow = arF[0:1, 3072:4096]
        abc = fs(4, 5)
        bbc = fs(5, 6)
        sqb = [fs(6, 7), fs(7, 8)]
        denrows = [prow2[0:1, 0:1024], prow2[0:1, 1024:2048]]
        rdens = [rbf[0:1, 0:1024], rbf[0:1, 1024:2048]]
        xtm = [arF[:, 0:2048], arF[:, 2048:4096]]
        gsil = fs(0, 1)
        gtmp = fs(1, 2)

        onesrow = onesb[0:1, 0:128]
        onescol = onesb[:, 0:1]
        d_row = srow[0:1, 0:16]
        lnd_row = srow[0:1, 16:32]
        coef2 = srow[0:1, 48:64]
        gv_row = srow[0:1, 80:88]
        sel_row = srow[0:1, 88:96]
        eps_ln_c = srow[0:1, 96:97]
        eps_38_c = srow[0:1, 97:98]
        lnd2 = [srow[0:1, 112:128], srow[0:1, 128:144]]

        bank_free = [None] * 8
        bank_ready = [None] * 8

        def bank_wait(i):
            return [bank_free[i]] if bank_free[i] is not None else []

        wplan = []
        for h in range(KC):
            wplan.append((qkv_t[h], 2048))
            wplan.append((qkv_t[16 + h], 2048))
            wplan.append((qkv_t[32 + h], 2048))
        for ncol in range(16):
            wplan.append((wout_t[ncol], 2048))
        moe_jobs = [dict(g=swg_t, u=swu_t, d=swd_t[0], comb=None, goff=0),
                    dict(g=swg_t, u=swu_t, d=swd_t[1], comb=None, goff=11)]
        for e in range(8):
            moe_jobs.append(dict(g=wg_t[e], u=wu_t[e], d=wd_t[e], comb=e,
                                 goff=0))
        for job in moe_jobs:
            for hc in range(HC_E):
                wplan.append((job["g"][job["goff"] + hc], 2048))
                wplan.append((job["u"][job["goff"] + hc], 2048))
            for ncol in range(16):
                wplan.append((job["d"][ncol], 1408))

        ring = {"issued": 0, "consumed": 0}
        slot_free = [None] * NSLOT
        slot_count = [0] * NSLOT

        def ring_issue():
            i = ring["issued"]
            if i >= len(wplan):
                return
            slot = i % NSLOT
            src, width = wplan[i]
            waits = [slot_free[slot]] if slot_free[slot] is not None else []
            dst = wbuf[:, slot * 2048: slot * 2048 + width]
            E("g", lambda g, s=src, d=dst: g.dma_start(out=d, in_=s),
              waits=waits, inc=sWs[slot], inc_amt=16, dma=True)
            slot_count[slot] += 1
            ring["issued"] = i + 1

        def ring_consume():
            i = ring["consumed"]
            slot = i % NSLOT
            ring["consumed"] = i + 1
            return slot, (sWs[slot], 16 * slot_count[slot])

        def ring_done(slot):
            slot_free[slot] = (sT, sT.n)
            ring_issue()

        def wtile(slot, kc, ncolw=128):
            return wbuf[:, slot * 2048 + kc * ncolw: slot * 2048 + (kc + 1) * ncolw]

        E("g", lambda g: g.dma_start(out=onesb[:, :], in_=ones_c), inc=sMisc,
          inc_amt=16, dma=True)
        E("g", lambda g: g.dma_start(out=identb[:, :], in_=ident_c), inc=sMisc,
          inc_amt=16, dma=True)
        E("g", lambda g: g.dma_start(out=seleb[:, :], in_=sele_c), inc=sMisc,
          inc_amt=16, dma=True)
        E("g", lambda g: g.dma_start(out=d_row, in_=decay.unsqueeze(0)),
          inc=sMisc, inc_amt=16, dma=True)
        E("g", lambda g: g.dma_start(out=srow[0:1, 80:96], in_=percore),
          inc=sMisc, inc_amt=16, dma=True)
        E("g", lambda g: g.dma_start(out=gwslot[:, :], in_=gate_t),
          inc=sMisc, inc_amt=16, dma=True)
        E("g", lambda g: g.dma_start(out=iotarow[:, :], in_=iota_c),
          inc=sMisc, inc_amt=16, dma=True)
        E("g", lambda g: g.dma_start(out=onesbf[:, :], in_=ones_c),
          inc=sMisc, inc_amt=16, dma=True)
        misc_done = (sMisc, sMisc.n)

        E("v", lambda v: v.memset(srow[0:1, 96:97], EPS_LN), waits=[misc_done])
        E("v", lambda v: v.memset(srow[0:1, 97:98], 1e-38))
        E("v", lambda v: v.memset(sin[:, :], 0.0), inc=sV)
        sin_zeroed = (sV, sV.n)

        E("s", lambda s: s.activation(d_row, d_row, AF.Sigmoid),
          waits=[misc_done, sin_zeroed])
        E("s", lambda s: s.activation(lnd_row, d_row, AF.Ln, bias=eps_38_c),
          inc=sA)
        dprep = (sA, sA.n)

        E("t", lambda t: t.matmul(pb[7][:, 0:16], onesrow, d_row, start=True,
                                  stop=True), waits=[dprep], inc=sT)
        E("s", lambda s: s.copy(dcol[:, :], pb[7][:, 0:16]),
          waits=[(sT, sT.n)], inc=sA)
        dcol_done = (sA, sA.n)
        bank_free[7] = dcol_done

        xtm_cons = [0, 0]
        for c in range(8):
            par = c % 2
            waits = [(sT, xtm_cons[par])] if c >= 2 else []
            E("y", lambda y, c=c, par=par: y.dma_start(
                out=xtm[par][:, :],
                in_=x_in.rearrange("(c p) f -> c p f", p=128)[c]),
              waits=waits, inc=sXin[par], inc_amt=16)
            xin_wait = (sXin[par], sXin[par].n)
            for b4 in range(4):
                bk = b4
                waits_t = [xin_wait] + bank_wait(bk)
                for i4 in range(4):
                    kc = b4 * 4 + i4
                    E("t", lambda t, par=par, kc=kc, bk=bk, i4=i4:
                      t.transpose(pb[bk][:, i4 * 128:(i4 + 1) * 128],
                                  xtm[par][:, kc * 128:(kc + 1) * 128],
                                  identb[:, :]),
                      waits=(waits_t if i4 == 0 else []),
                      inc=(sT if i4 == 3 else None))
                bank_ready[bk] = (sT, sT.n)
                dst = xres.rearrange("p (kc t) -> p kc t", t=TC)[
                    :, b4 * 4:(b4 + 1) * 4, c * 128:(c + 1) * 128]
                src = pb[bk][:, 0:512].rearrange("p (i j) -> p i j", j=128)
                E("s", lambda s, dst=dst, src=src: s.copy(dst, src),
                  waits=[bank_ready[bk]], inc=sA)
                bank_free[bk] = (sA, sA.n)
            xtm_cons[par] = sT.n
        xin_done = (sA, sA.n)

        for _ in range(NSLOT):
            ring_issue()

        def emit_ln(ready_wait):
            if not isinstance(ready_wait, list):
                ready_wait = [ready_wait] * KC
            stat_banks = {0: 4, 1: 5}
            sq_banks = {0: 0, 1: 1}
            sq_cons = [0, 0]
            sq_keys = ("s", "v", "g")
            for kc in range(KC):
                sq = sqb[kc % 2]
                sq_wait = [ready_wait[kc]]
                if kc >= 2:
                    sq_wait.append((sT, sq_cons[kc % 2]))
                skey = sq_keys[kc % 3]
                if skey == "s":
                    E("s", lambda s, kc=kc, sq=sq: s.square(
                        sq[:, :], xres[:, kc * TC:(kc + 1) * TC]),
                      waits=sq_wait)
                else:
                    E(skey, lambda e, kc=kc, sq=sq: e.tensor_mul(
                        sq[:, :], xres[:, kc * TC:(kc + 1) * TC],
                        xres[:, kc * TC:(kc + 1) * TC]),
                      waits=sq_wait)
                es = sch.esem[skey]
                sq_ready = (es, es.n)
                for th in range(2):
                    bkx = stat_banks[th]
                    w = [ready_wait[kc]] + (bank_wait(bkx) if kc == 0 else [])
                    E("t", lambda t, kc=kc, th=th, bkx=bkx: t.matmul(
                        pb[bkx][0:1, :], onescol,
                        xres[:, kc * TC + th * 512: kc * TC + (th + 1) * 512],
                        start=(kc == 0), stop=(kc == KC - 1)),
                      waits=w, inc=(sT if kc == KC - 1 else None))
                    bkq = sq_banks[th]
                    w = [sq_ready] + (bank_wait(bkq) if kc == 0 else [])
                    E("t", lambda t, th=th, bkq=bkq, sq=sq: t.matmul(
                        pb[bkq][0:1, :], onescol,
                        sq[:, th * 512:(th + 1) * 512],
                        start=(kc == 0), stop=(kc == KC - 1)),
                      waits=w, inc=(sT if (kc == KC - 1 or th == 1) else None))
                sq_cons[kc % 2] = sT.n
            stats_done = (sT, sT.n)
            E("v", lambda v: v.tensor_copy(mrow[:, 0:512], pb[4][0:1, :]),
              waits=[stats_done])
            E("v", lambda v: v.tensor_copy(mrow[:, 512:1024], pb[5][0:1, :]))
            E("v", lambda v: v.tensor_copy(varrow[:, 0:512], pb[0][0:1, :]))
            E("v", lambda v: v.tensor_copy(varrow[:, 512:1024], pb[1][0:1, :]))
            E("v", lambda v: v.tensor_scalar_mul(mrow, mrow, 1.0 / DM))
            E("v", lambda v: v.tensor_scalar_mul(varrow, varrow, 1.0 / DM))
            E("v", lambda v: v.tensor_mul(arow, mrow, mrow))
            E("v", lambda v: v.tensor_sub(varrow, varrow, arow), inc=sV)
            var_done = (sV, sV.n)
            for i in (0, 1, 4, 5):
                bank_free[i] = var_done
            E("s", lambda s: s.activation(arow, varrow, AF.Ln, bias=eps_ln_c),
              waits=[var_done])
            E("s", lambda s: s.activation(arow, arow, AF.Exp, scale=-0.5),
              inc=sA)
            rstd_done = (sA, sA.n)
            E("v", lambda v: v.tensor_mul(brow, mrow, arow), waits=[rstd_done],
              inc=sV)
            rows_done = (sV, sV.n)
            for bk, row, dst in ((2, arow, abc), (3, brow, bbc)):
                for th in range(2):
                    bki = bk + 4 * th
                    E("t", lambda t, bki=bki, row=row, th=th: t.matmul(
                        pb[bki][:, :], onesrow,
                        row[0:1, th * 512:(th + 1) * 512],
                        start=True, stop=True),
                      waits=[rows_done] + bank_wait(bki), inc=sT)
                    bank_ready[bki] = (sT, sT.n)
                    E("s", lambda s, bki=bki, dst=dst, th=th: s.copy(
                        dst[:, th * 512:(th + 1) * 512], pb[bki][:, :]),
                      waits=[bank_ready[bki]], inc=sA)
                    bank_free[bki] = (sA, sA.n)
            bcast_done = (sA, sA.n)
            marks = []
            for kc in range(KC):
                key = "g" if kc % 3 == 2 else "v"
                sem = sP if key == "g" else sV
                scratch = sqb[1] if key == "g" else sqb[0]
                E(key, lambda e, kc=kc, scratch=scratch: e.tensor_mul(
                    scratch[:, :], xres[:, kc * TC:(kc + 1) * TC], abc[:, :]),
                  waits=[bcast_done])
                E(key, lambda e, kc=kc, scratch=scratch: e.tensor_sub(
                    xhat[:, kc * TC:(kc + 1) * TC], scratch[:, :], bbc[:, :]),
                  inc=sem)
                marks.append((sem, sem.n))
            return marks, (sV, sV.n), (sP, sP.n)

        ln1_marks, ln1_v, ln1_p = emit_ln(xin_done)
        if DEBUG:
            E("y", lambda y: y.dma_start(out=dbg_xhat1, in_=xhat[:, :]),
              waits=[ln1_v, ln1_p], inc=sDbg, inc_amt=16)

        qb = [S0, S1]
        kb = [S2, S3]
        vb = [S4, S5]
        ykvb = [S6, sqb[1]]
        qkv_pairs = [(0, 1), (2, 3), (4, 5)]
        head_buf_free = {0: None, 1: None}
        sbar_marks = [None] * KC

        for half in range(2):
            for twoi in range(2):
                E("v", lambda v, half=half, twoi=twoi: v.tensor_copy(
                    lnd2[half].rearrange("o (i two) -> o two i", two=2)[
                        :, twoi, :],
                    lnd_row[0:1, half * 8:(half + 1) * 8]),
                  waits=([dprep] if half == 0 and twoi == 0 else []),
                  inc=(sV if half == 1 and twoi == 1 else None))
        lnd2_done = (sV, sV.n)

        def emit_gather_issue(half, sbar_mark):
            cin = cc_inA if half == 0 else cc_inB
            cout = cc_outA if half == 0 else cc_outB
            E("g", lambda g, cin=cin, half=half: g.dma_start(
                out=cin, in_=sbar[:, half * 16:(half + 1) * 16]),
              waits=[sbar_mark], inc=sMisc, inc_amt=16, dma=True)
            E("g", lambda g, cin=cin, cout=cout: g.collective_compute(
                "AllGather", OP.bypass, replica_groups=[list(range(NCORES))],
                ins=[cin.opt()], outs=[cout.opt()]),
              waits=[(sMisc, sMisc.n)], inc=sCC, inc_amt=1, dma=True)
            return (sCC, sCC.n)

        def emit_gather_load(half, cc_mark, queue="g"):
            cout = cc_outA if half == 0 else cc_outB
            E(queue, lambda g, cout=cout, half=half: g.dma_start(
                out=gath[:, half * 128:(half + 1) * 128].rearrange(
                    "p (r c) -> p r c", c=16),
                in_=cout.rearrange("(r p) c -> p r c", p=128)),
              waits=[cc_mark], inc=sMisc, inc_amt=16, dma=True)
            return (sMisc, sMisc.n)

        def emit_sin(half, gw_done):
            pe_cons = None
            for r in range(NCORES):
                E("s", lambda s, half=half, r=r: s.activation(
                    coef2, lnd2[half], AF.Exp, scale=gv_row[0:1, r:r + 1]),
                  waits=([lnd2_done] if r == 0 else [pe_cons]))
                E("v", lambda v, r=r: v.tensor_scalar_mul(
                    coef2, coef2, sel_row[0:1, r:r + 1]),
                  waits=[(sA, sA.n)], inc=sV)
                cf = (sV, sV.n)
                bk = 6 if r % 2 == 0 else 7
                E("t", lambda t, bk=bk: t.matmul(pb[bk][:, 0:16], onesrow,
                                                 coef2, start=True, stop=True),
                  waits=[cf] + bank_wait(bk), inc=sT)
                pe_cons = (sT, sT.n)
                bank_ready[bk] = pe_cons
                E("v", lambda v, r=r, bk=bk, half=half: v.tensor_mul(
                    scr32[:, 0:16],
                    gath[:, half * 128 + r * 16: half * 128 + (r + 1) * 16],
                    pb[bk][:, 0:16]),
                  waits=[bank_ready[bk], gw_done, sin_zeroed])
                E("v", lambda v, half=half: v.tensor_add(
                    sin[:, half * 16:(half + 1) * 16],
                    sin[:, half * 16:(half + 1) * 16], scr32[:, 0:16]),
                  inc=sV)
                bank_free[bk] = (sV, sV.n)
            return (sV, sV.n)

        gath_done = [None, None]
        sin_done2 = [None, None]

        for h in range(KC):
            par = h % 2
            col_psum = {}
            for ci, role in enumerate(("q", "k", "v")):
                slot, wl_wait = ring_consume()
                pr = qkv_pairs[ci]
                waits = [wl_wait]
                waits += bank_wait(pr[0]) + bank_wait(pr[1])
                for kc in range(KC):
                    kw = list(waits) if kc == 0 else []
                    if h < 2:
                        kw.append(ln1_marks[kc])
                    for th in range(2):
                        E("t", lambda t, kc=kc, th=th, pr=pr, slot=slot:
                          t.matmul(pb[pr[th]][:, :], wtile(slot, kc),
                                   xhat[:, kc * TC + th * 512:
                                        kc * TC + (th + 1) * 512],
                                   start=(kc == 0), stop=(kc == KC - 1)),
                          waits=(kw if th == 0 else []),
                          inc=(sT if kc == KC - 1 and th == 1 else None))
                ring_done(slot)
                col_psum[role] = (pr, (sT, sT.n))
            bufw = [head_buf_free[par]] if head_buf_free[par] else []
            phi_done = {}
            for role, buf in (("q", qb[par]), ("k", kb[par])):
                pr, rdy = col_psum[role]
                exm = []
                for th in range(2):
                    sl = buf[:, th * 512:(th + 1) * 512]
                    E("v", lambda v, pr=pr, th=th, sl=sl:
                      v.tensor_single_scalar(sl, pb[pr[th]][:, :], 0.0, OP.min),
                      waits=([rdy] + bufw if th == 0 else []), inc=sV)
                    mn = (sV, sV.n)
                    E("s", lambda s, sl=sl: s.activation(sl, sl, AF.Exp),
                      waits=[mn], inc=sA)
                    exm.append((sA, sA.n))
                for th in range(2):
                    sl = buf[:, th * 512:(th + 1) * 512]
                    E("v", lambda v, pr=pr, th=th, sl=sl:
                      v.scalar_tensor_tensor(sl, pb[pr[th]][:, :], 0.0, sl,
                                             OP.max, OP.add),
                      waits=[exm[th]], inc=sV)
                phi_done[role] = (sV, sV.n)
                bank_free[pr[0]] = phi_done[role]
                bank_free[pr[1]] = phi_done[role]
            pr, rdy = col_psum["v"]
            for th in range(2):
                E("s", lambda s, pr=pr, th=th, par=par: s.copy(
                    vb[par][:, th * 512:(th + 1) * 512], pb[pr[th]][:, :]),
                  waits=([rdy] + bufw if th == 0 else []), inc=sA)
            v_done = (sA, sA.n)
            bank_free[pr[0]] = v_done
            bank_free[pr[1]] = v_done
            E("v", lambda v, par=par: v.tensor_mul(
                vb[par][:, :], kb[par][:, :], vb[par][:, :]),
              waits=[v_done], inc=sV)
            kv_done = (sV, sV.n)
            scan1_w = [dcol_done] + bufw
            if h < 2:
                scan1_w.append(ln1_p)
            E("v", lambda v, h=h, par=par: v.tensor_tensor_scan(
                ykvb[par][:, :], dcol[:, h:h + 1].broadcast_to((128, TC)),
                vb[par][:, :], 0.0, OP.mult, OP.add), waits=scan1_w)
            E("v", lambda v, h=h, par=par: v.tensor_copy(
                sbar[:, 2 * h:2 * h + 1], ykvb[par][:, 1023:1024]))
            E("v", lambda v, h=h, par=par: v.tensor_tensor_scan(
                vb[par][:, :], dcol[:, h:h + 1].broadcast_to((128, TC)),
                kb[par][:, :], 0.0, OP.mult, OP.add))
            E("v", lambda v, h=h, par=par: v.tensor_copy(
                sbar[:, 2 * h + 1:2 * h + 2], vb[par][:, 1023:1024]), inc=sV)
            sbar_marks[h] = (sV, sV.n)
            E("y", lambda y, h=h, par=par: y.dma_start(
                out=sp_q.rearrange("(hh p) t -> hh p t", p=128)[h],
                in_=qb[par][:, :]), waits=[phi_done["q"]], inc=sSp, inc_amt=16)
            E("y", lambda y, h=h, par=par: y.dma_start(
                out=sp_ykv.rearrange("(hh p) t -> hh p t", p=128)[h],
                in_=ykvb[par][:, :]), waits=[sbar_marks[h]], inc=sSp,
              inc_amt=16)
            E("y", lambda y, h=h, par=par: y.dma_start(
                out=sp_yk.rearrange("(hh p) t -> hh p t", p=128)[h],
                in_=vb[par][:, :]), waits=[], inc=sSp, inc_amt=16)
            head_buf_free[par] = (sSp, sSp.n)
            if h == 7:
                ccA = emit_gather_issue(0, sbar_marks[7])
            if h == 13:
                gath_done[0] = emit_gather_load(0, ccA)
                sin_done2[0] = emit_sin(0, gath_done[0])
        spill_all = (sSp, sSp.n)
        ccB = emit_gather_issue(1, sbar_marks[15])

        p3sets = [(S0, S1, S2), (S3, S4, S5), (S6, S7, S8)]
        p3_slot_free = {0: None, 1: None, 2: None}
        dd_banks = [(0, 1), (2, 3)]
        bc_banks = [(4, 5), (6, 7)]
        p3_state = {"dp_cons": None}
        corr_marks = [None] * KC

        def p3_stage1(h):
            si = h % 3
            bq, by1, by2 = p3sets[si]
            waits_in = [spill_all]
            if p3_slot_free[si]:
                waits_in.append(p3_slot_free[si])
            for src, dst in ((sp_q, bq), (sp_ykv, by1), (sp_yk, by2)):
                E("y", lambda y, src=src, dst=dst, h=h: y.dma_start(
                    out=dst[:, :],
                    in_=src.rearrange("(hh p) t -> hh p t", p=128)[h]),
                  waits=waits_in, inc=sIn[si], inc_amt=16)
                waits_in = []
            in_done = (sIn[si], sIn[si].n)
            E("s", lambda s, h=h: s.activation(
                dprow[0:1, :], iotarow[0:1, :], AF.Exp,
                scale=lnd_row[0:1, h:h + 1]),
              waits=([misc_done, dprep] if p3_state["dp_cons"] is None
                     else [p3_state["dp_cons"]]))
            dp_done = (sA, sA.n)
            dbk = dd_banks[h % 2]
            for th in range(2):
                E("t", lambda t, th=th, dbk=dbk: t.matmul(
                    pb[dbk[th]][:, :], onesbf[0:1, :],
                    dprow[0:1, th * 512:(th + 1) * 512],
                    start=True, stop=True),
                  waits=([dp_done] + bank_wait(dbk[0]) + bank_wait(dbk[1])
                         if th == 0 else []),
                  inc=(sT if th == 1 else None))
            dpmm = (sT, sT.n)
            p3_state["dp_cons"] = dpmm
            sin_w = sin_done2[0] if h < 8 else sin_done2[1]
            first = [dpmm, in_done, sin_w]
            for ci, ybuf in ((0, by1), (1, by2)):
                for th in range(2):
                    E("v", lambda v, ci=ci, ybuf=ybuf, th=th, h=h, dbk=dbk:
                      v.scalar_tensor_tensor(
                        ybuf[:, th * 512:(th + 1) * 512], pb[dbk[th]][:, :],
                        sin[:, 2 * h + ci:2 * h + ci + 1],
                        ybuf[:, th * 512:(th + 1) * 512], OP.mult, OP.add),
                      waits=first,
                      inc=(sV if ci == 1 and th == 1 else None))
                    first = []
            corr_marks[h] = (sV, sV.n)

        def p3_stage2(h):
            si = h % 3
            bq, by1, by2 = p3sets[si]
            dbk = dd_banks[h % 2]
            bcb = bc_banks[h % 2]
            dnr = denrows[h % 2]
            rdn = rdens[h % 2]
            prodsl = xhat[:, h * TC:(h + 1) * TC]
            E("v", lambda v, bq=bq, by2=by2, prodsl=prodsl: v.tensor_mul(
                prodsl, bq[:, :], by2[:, :]),
              waits=[corr_marks[h]], inc=sV)
            prod_done = (sV, sV.n)
            for th in range(2):
                E("t", lambda t, th=th, dbk=dbk, prodsl=prodsl: t.matmul(
                    pb[dbk[th]][0:1, :], onesbf[:, 0:1],
                    prodsl[:, th * 512:(th + 1) * 512],
                    start=True, stop=True),
                  waits=([prod_done] if th == 0 else []),
                  inc=(sT if th == 1 else None))
            den_mm = (sT, sT.n)
            for th in range(2):
                E("v", lambda v, th=th, dbk=dbk, dnr=dnr:
                  v.tensor_single_scalar(
                    dnr[0:1, th * 512:(th + 1) * 512], pb[dbk[th]][0:1, :],
                    EPS_DEN, OP.max),
                  waits=([den_mm] if th == 0 else []),
                  inc=(sV if th == 1 else None))
            den_row_done = (sV, sV.n)
            bank_free[dbk[0]] = den_row_done
            bank_free[dbk[1]] = den_row_done
            E("s", lambda s, dnr=dnr: s.activation(dnr, dnr, AF.Ln),
              waits=[den_row_done])
            E("s", lambda s, rdn=rdn, dnr=dnr: s.activation(
                rdn, dnr, AF.Exp, scale=-1.0), inc=sA)
            rden_done = (sA, sA.n)
            for th in range(2):
                E("t", lambda t, th=th, bcb=bcb, rdn=rdn: t.matmul(
                    pb[bcb[th]][:, :], onesbf[0:1, :],
                    rdn[0:1, th * 512:(th + 1) * 512], start=True, stop=True),
                  waits=([rden_done] + bank_wait(bcb[0]) + bank_wait(bcb[1])
                         if th == 0 else []),
                  inc=(sT if th == 1 else None))
            bc_done = (sT, sT.n)
            E("g", lambda g, bq=bq, by1=by1: g.tensor_mul(
                bq[:, :], bq[:, :], by1[:, :]),
              waits=[corr_marks[h], prod_done], inc=sP)
            qm_done = (sP, sP.n)
            for th in range(2):
                E("v", lambda v, th=th, h=h, bq=bq, bcb=bcb: v.tensor_mul(
                    xhat[:, h * TC + th * 512: h * TC + (th + 1) * 512],
                    bq[:, th * 512:(th + 1) * 512], pb[bcb[th]][:, :]),
                  waits=([bc_done, qm_done] if th == 0 else []),
                  inc=(sV if th == 1 else None))
            outpre_done = (sV, sV.n)
            bank_free[bcb[0]] = outpre_done
            bank_free[bcb[1]] = outpre_done
            p3_slot_free[si] = outpre_done

        for h in range(KC):
            if h == 7:
                gath_done[1] = emit_gather_load(1, ccB, queue="y")
            if h == 8:
                p3_stage2(6)
                sin_done2[1] = emit_sin(1, gath_done[1])
            elif h >= 2:
                p3_stage2(h - 2)
            p3_stage1(h)
        p3_stage2(KC - 2)
        p3_stage2(KC - 1)
        outpre_all = (sV, sV.n)
        if DEBUG:
            E("y", lambda y: y.dma_start(out=dbg_outpre, in_=xhat[:, :]),
              waits=[outpre_all], inc=sDbg, inc_amt=16)
            E("y", lambda y: y.dma_start(out=dbg_small[:, 0:32],
                                         in_=sbar[:, :]),
              waits=[sbar_marks[15]], inc=sDbg, inc_amt=16)
            E("y", lambda y: y.dma_start(out=dbg_small[:, 32:64],
                                         in_=sin[:, :]),
              waits=[sin_done2[1]], inc=sDbg, inc_amt=16)
            E("y", lambda y: y.dma_start(out=dbg_small[:, 64:320],
                                         in_=gath[:, :]),
              waits=[gath_done[1]], inc=sDbg, inc_amt=16)
            E("y", lambda y: y.dma_start(out=dbg_small[:, 320:336],
                                         in_=dcol[:, :]),
              waits=[dcol_done], inc=sDbg, inc_amt=16)
            E("y", lambda y: y.dma_start(out=dbg_small[0:1, 336:464],
                                         in_=srow[0:1, 0:128]),
              waits=[sin_done2[1]], inc=sDbg, inc_amt=16)

        op_pairs = [(0, 1), (2, 3)]
        xres_marks = []
        for ncol in range(16):
            slot, wl_wait = ring_consume()
            pr = op_pairs[ncol % 2]
            waits = [wl_wait, outpre_all] + bank_wait(pr[0]) + bank_wait(pr[1])
            for kc in range(KC):
                for th in range(2):
                    E("t", lambda t, kc=kc, th=th, pr=pr, slot=slot: t.matmul(
                        pb[pr[th]][:, :], wtile(slot, kc),
                        xhat[:, kc * TC + th * 512: kc * TC + (th + 1) * 512],
                        start=(kc == 0), stop=(kc == KC - 1)),
                      waits=(waits if kc == 0 and th == 0 else []),
                      inc=(sT if kc == KC - 1 and th == 1 else None))
            ring_done(slot)
            rdy = (sT, sT.n)
            for th in range(2):
                sl = xres[:, ncol * TC + th * 512: ncol * TC + (th + 1) * 512]
                E("v", lambda v, sl=sl, pr=pr, th=th: v.tensor_add(
                    sl, sl, pb[pr[th]][:, :]),
                  waits=([rdy] if th == 0 else []),
                  inc=(sV if th == 1 else None))
            ev = (sV, sV.n)
            bank_free[pr[0]] = ev
            bank_free[pr[1]] = ev
            xres_marks.append(ev)
        attn_done = (sV, sV.n)
        if DEBUG:
            E("y", lambda y: y.dma_start(out=dbg_xres, in_=xres[:, :]),
              waits=[attn_done], inc=sDbg, inc_amt=16)

        ln2_marks, ln2_v, ln2_p = emit_ln(xres_marks)

        mx = scr8[:, 0:1]
        mxn = scr8[:, 1:2]
        sume = scr8[:, 2:3]
        rsum = scr8[:, 3:4]
        escr = scr8[:, 8:16]
        cnt = scr8[:, 16:24]
        tmp8 = scr8[:, 24:32]
        comb_marks = [None] * 8
        for tc in range(8):
            bk = 4 + tc % 4
            scomb = scr8[:, 64 + tc * 8: 72 + tc * 8]
            waits = [misc_done] + bank_wait(bk)
            for kc in range(KC):
                w = []
                if tc < 2:
                    w.append(ln2_marks[kc])
                if kc == 0:
                    w = w + waits
                E("t", lambda t, kc=kc, tc=tc, bk=bk: t.matmul(
                    pb[bk][:, 0:8],
                    xhat[:, kc * TC + tc * 128: kc * TC + (tc + 1) * 128],
                    gwslot[:, kc * 8:(kc + 1) * 8],
                    start=(kc == 0), stop=(kc == KC - 1)),
                  waits=w, inc=(sT if kc == KC - 1 else None))
            lg = (sT, sT.n)
            E("v", lambda v, bk=bk: v.reduce_max(mx, pb[bk][:, 0:8], axis=AX),
              waits=[lg])
            E("v", lambda v: v.tensor_scalar_mul(mxn, mx, -1.0), inc=sV)
            E("s", lambda s, bk=bk: s.activation(escr, pb[bk][:, 0:8], AF.Exp,
                                                 bias=mxn),
              waits=[(sV, sV.n)], inc=sA)
            ex_done = (sA, sA.n)
            bank_free[bk] = ex_done
            E("v", lambda v: v.reduce_sum(sume, escr, axis=AX), waits=[ex_done])
            E("v", lambda v: v.reciprocal(rsum, sume))
            E("v", lambda v: v.tensor_scalar_mul(escr, escr, rsum))
            E("v", lambda v: v.memset(cnt, 0.0))
            for j in range(8):
                E("v", lambda v, j=j: v.tensor_single_scalar(
                    tmp8, escr, escr[:, j:j + 1], OP.is_lt))
                E("v", lambda v: v.tensor_add(cnt, cnt, tmp8))
            E("v", lambda v: v.tensor_single_scalar(tmp8, cnt, 1.5, OP.is_lt))
            E("v", lambda v, scomb=scomb: v.tensor_mul(scomb, escr, tmp8),
              inc=sV)
            comb_marks[tc] = (sV, sV.n)

        def emit_combT():
            for tc in range(8):
                scomb = scr8[:, 64 + tc * 8: 72 + tc * 8]
                bkt = 6 if tc % 2 == 0 else 7
                E("t", lambda t, bkt=bkt, scomb=scomb: t.transpose(
                    pb[bkt][0:8, 0:128], scomb, identb[:, :]),
                  waits=[comb_marks[tc]] + bank_wait(bkt), inc=sT)
                tr_done = (sT, sT.n)
                E("s", lambda s, tc=tc, bkt=bkt: s.copy(
                    combT[:, tc * 128:(tc + 1) * 128], pb[bkt][0:8, 0:128]),
                  waits=[tr_done], inc=sA)
                bank_free[bkt] = (sA, sA.n)
            return (sA, sA.n)

        gu_banks = (0, 1, 2, 3)
        down_pairs = [(4, 5), (6, 7)]
        prev_gmark = None
        combT_done = None
        for ji, job in enumerate(moe_jobs):
            if ji == 1:
                combT_done = emit_combT()
            combi = job["comb"]
            cslot = ji % 2
            if combi is not None:
                for th in range(2):
                    bk = gu_banks[2 + th]
                    E("t", lambda t, combi=combi, th=th, bk=bk: t.matmul(
                        pb[bk][:, :], seleb[:, combi * 128:(combi + 1) * 128],
                        combT[:, th * 512:(th + 1) * 512],
                        start=True, stop=True),
                      waits=[combT_done] + bank_wait(bk), inc=sT)
                    bank_ready[bk] = (sT, sT.n)
                    E("s", lambda s, th=th, bk=bk, cslot=cslot: s.copy(
                        combbc[:, cslot * TC + th * 512:
                               cslot * TC + (th + 1) * 512],
                        pb[bk][:, :]),
                      waits=[bank_ready[bk]], inc=sA)
                    bank_free[bk] = (sA, sA.n)
            combbc_done = (sA, sA.n)
            g_all = None
            ev_prev = None
            for hc in range(HC_E):
                gub = (0, 1, 2, 3) if hc % 2 == 0 else (4, 5, 6, 7)
                sg, wg_wait = ring_consume()
                su, wu_wait = ring_consume()
                waits0 = [wg_wait, wu_wait]
                if ji > 0:
                    waits0.append(ln2_v)
                    waits0.append(ln2_p)
                for b in gub:
                    waits0 += bank_wait(b)
                for wi, (slot, b0) in enumerate(((sg, 0), (su, 2))):
                    for th in range(2):
                        bk = gub[b0 + th]
                        for kc in range(KC):
                            kw = (list(waits0) if wi == 0 and th == 0
                                  and kc == 0 else [])
                            if ji == 0 and hc == 0 and wi == 0 and th == 0:
                                kw.append(ln2_marks[kc])
                            E("t", lambda t, kc=kc, bk=bk, th=th, slot=slot:
                              t.matmul(pb[bk][:, :], wtile(slot, kc),
                                       xhat[:, kc * TC + th * 512:
                                            kc * TC + (th + 1) * 512],
                                       start=(kc == 0), stop=(kc == KC - 1)),
                              waits=kw,
                              inc=(sT if kc == KC - 1 else None))
                ring_done(sg)
                ring_done(su)
                gu_rdy = (sT, sT.n)
                gw = [gu_rdy]
                if ev_prev is not None:
                    gw.append(ev_prev)
                if prev_gmark is not None:
                    gw.append(prev_gmark)
                if combi is not None:
                    gw.append(combbc_done)
                for th in range(2):
                    E("s", lambda s, th=th, gub=gub: s.activation(
                        gsil[:, th * 512:(th + 1) * 512],
                        pb[gub[th]][:, :], AF.Silu),
                      waits=(gw if th == 0 else []), inc=sA)
                    sil = (sA, sA.n)
                    gdst = gar[:, hc * TC + th * 512: hc * TC + (th + 1) * 512]
                    if combi is not None:
                        E("v", lambda v, th=th, gub=gub: v.scalar_tensor_tensor(
                            gtmp[:, th * 512:(th + 1) * 512],
                            pb[gub[2 + th]][:, :], 0.0,
                            gsil[:, th * 512:(th + 1) * 512],
                            OP.add, OP.mult),
                          waits=[sil])
                        E("v", lambda v, th=th, gdst=gdst, cslot=cslot:
                          v.tensor_mul(
                            gdst, gtmp[:, th * 512:(th + 1) * 512],
                            combbc[:, cslot * TC + th * 512:
                                   cslot * TC + (th + 1) * 512]),
                          inc=sV)
                    else:
                        E("v", lambda v, th=th, gdst=gdst, gub=gub:
                          v.scalar_tensor_tensor(
                            gdst, pb[gub[2 + th]][:, :], 0.0,
                            gsil[:, th * 512:(th + 1) * 512],
                            OP.add, OP.mult),
                          waits=[sil], inc=sV)
                ev = (sV, sV.n)
                ev_prev = ev
                g_all = ev
                for b in gub:
                    bank_free[b] = ev
            for ncol in range(16):
                slot, wd_wait = ring_consume()
                pr = down_pairs[ncol % 2]
                waits = [wd_wait, g_all] + bank_wait(pr[0]) + bank_wait(pr[1])
                for kc in range(HC_E):
                    for th in range(2):
                        E("t", lambda t, kc=kc, th=th, pr=pr, slot=slot:
                          t.matmul(pb[pr[th]][:, :], wtile(slot, kc),
                                   gar[:, kc * TC + th * 512:
                                       kc * TC + (th + 1) * 512],
                                   start=(kc == 0), stop=(kc == HC_E - 1)),
                          waits=(waits if kc == 0 and th == 0 else []),
                          inc=(sT if kc == HC_E - 1 and th == 1 else None))
                ring_done(slot)
                rdy = (sT, sT.n)
                for th in range(2):
                    sl = xres[:, ncol * TC + th * 512:
                              ncol * TC + (th + 1) * 512]
                    E("v", lambda v, sl=sl, pr=pr, th=th: v.tensor_add(
                        sl, sl, pb[pr[th]][:, :]),
                      waits=([rdy] if th == 0 else []),
                      inc=(sV if th == 1 else None))
                ev = (sV, sV.n)
                bank_free[pr[0]] = ev
                bank_free[pr[1]] = ev
            prev_gmark = (sT, sT.n)
        moe_done = (sV, sV.n)

        for c in range(8):
            par = c % 2
            for b4 in range(4):
                bk = b4
                waits_t = [moe_done] + bank_wait(bk)
                if c >= 2 and b4 == 0:
                    waits_t.append((sOd[par], 16 * (c // 2)))
                for i4 in range(4):
                    kc = b4 * 4 + i4
                    E("t", lambda t, kc=kc, c=c, bk=bk, i4=i4: t.transpose(
                        pb[bk][:, i4 * 128:(i4 + 1) * 128],
                        xres[:, kc * TC + c * 128: kc * TC + (c + 1) * 128],
                        identb[:, :]),
                      waits=(waits_t if i4 == 0 else []),
                      inc=(sT if i4 == 3 else None))
                bank_ready[bk] = (sT, sT.n)
                dst = xtm[par].rearrange("p (kk j) -> p kk j", j=128)[
                    :, b4 * 4:(b4 + 1) * 4, :]
                src = pb[bk][:, 0:512].rearrange("p (i j) -> p i j", j=128)
                E("s", lambda s, dst=dst, src=src: s.copy(dst, src),
                  waits=[bank_ready[bk]], inc=sA)
                bank_free[bk] = (sA, sA.n)
            ev = (sA, sA.n)
            E("y", lambda y, c=c, par=par: y.dma_start(
                out=out_t.rearrange("(c p) f -> c p f", p=128)[c],
                in_=xtm[par][:, :]), waits=[ev], inc=sOd[par], inc_amt=16)
        final_waits = [(sOd[0], sOd[0].n), (sOd[1], sOd[1].n)]
        if DEBUG:
            final_waits.append((sDbg, sDbg.n))
        E("y", lambda y: None, waits=final_waits)

        assert ring["issued"] == len(wplan) and ring["consumed"] == len(wplan)

        block.tensor(sch.run_key("t"))
        block.vector(sch.run_key("v"))
        block.scalar(sch.run_key("s"))
        block.gpsimd(sch.run_key("g"))
        block.sync(sch.run_key("y"))

    return nc


_PROG_CACHE = {}


def _get_program():
    if "nc" not in _PROG_CACHE:
        _PROG_CACHE["nc"] = build_program()
    return _PROG_CACHE["nc"]


def tile_km(W, nkc, ncols):
    return np.ascontiguousarray(
        W.reshape(nkc, 128, ncols, 128).transpose(2, 1, 0, 3)
        .reshape(ncols, 128, nkc * 128))


def make_in_maps(L0_attn_x, L0_attn_w_qkv, L0_attn_w_out, L0_attn_decay_param,
                 L0_ffn_gate_w, L0_ffn_w_gate, L0_ffn_w_up, L0_ffn_w_down,
                 L0_ffn_sw_gate, L0_ffn_sw_up, L0_ffn_sw_down):
    x = np.ascontiguousarray(np.asarray(L0_attn_x, np.float32).reshape(8192, DM))
    dec = np.ascontiguousarray(np.asarray(L0_attn_decay_param, np.float32))
    qkv_t = tile_km(np.asarray(L0_attn_w_qkv, np.float32), 16, 48)
    wout_t = tile_km(np.asarray(L0_attn_w_out, np.float32), 16, 16)
    gw = np.asarray(L0_ffn_gate_w, np.float32)
    gate_t = np.ascontiguousarray(
        gw.reshape(16, 128, 8).transpose(1, 0, 2).reshape(128, 128))
    wg = np.asarray(L0_ffn_w_gate, np.float32)
    wu = np.asarray(L0_ffn_w_up, np.float32)
    wd = np.asarray(L0_ffn_w_down, np.float32)
    wg_t = np.stack([tile_km(wg[e], 16, 11) for e in range(8)])
    wu_t = np.stack([tile_km(wu[e], 16, 11) for e in range(8)])
    wd_t = np.stack([tile_km(wd[e], 11, 16) for e in range(8)])
    swg_t = tile_km(np.asarray(L0_ffn_sw_gate, np.float32), 16, 22)
    swu_t = tile_km(np.asarray(L0_ffn_sw_up, np.float32), 16, 22)
    swd = np.asarray(L0_ffn_sw_down, np.float32)
    swd_t = np.stack([tile_km(swd[0:1408], 11, 16),
                      tile_km(swd[1408:2816], 11, 16)])

    ones_c = np.ones((128, 128), np.float32)
    ident_c = np.eye(128, dtype=np.float32)
    iota = np.arange(1, 1025, dtype=np.float32).reshape(1, 1024)
    sele = np.zeros((8, 1024), np.float32)
    for e in range(8):
        sele[e, e * 128:(e + 1) * 128] = 1.0
    sele = sele.astype(ml_dtypes.bfloat16)

    in_maps = []
    for c in range(NCORES):
        batch0 = (c // 4) * 4
        pc = np.zeros((1, 16), np.float32)
        for g, cp in enumerate(range(c - 1, batch0 - 1, -1)):
            pc[0, cp] = 1024.0 * g
            pc[0, 8 + cp] = 1.0
        in_maps.append({
            "x": x[c * TC:(c + 1) * TC],
            "qkv_t": qkv_t, "wout_t": wout_t, "gate_t": gate_t,
            "wg_t": wg_t, "wu_t": wu_t, "wd_t": wd_t,
            "swg_t": swg_t, "swu_t": swu_t, "swd_t": swd_t,
            "decay": dec, "percore": pc,
            "ones_c": ones_c, "ident_c": ident_c, "sele_c": sele,
            "iota_c": iota,
        })
    return in_maps


def kernel(**inputs):
    in_maps = make_in_maps(**inputs)
    nc = _get_program()
    res = run_bass_kernel_spmd(nc, in_maps, core_ids=list(range(NCORES)))
    out = np.concatenate([res.results[c]["out"] for c in range(NCORES)], axis=0)
    return out.reshape(2, 4096, DM).astype(np.float32)


# version 30
# speedup vs baseline: 1.0131x; 1.0131x over previous
import numpy as np
import ml_dtypes
import concourse.bass as bass
import concourse.mybir as mybir
from concourse.bass_utils import run_bass_kernel_spmd

F32 = mybir.dt.float32
BF16 = mybir.dt.bfloat16
AF = mybir.ActivationFunctionType
OP = mybir.AluOpType
AX = mybir.AxisListType.X

NCORES = 8
TC = 1024
KC = 16
DM = 2048
HC_E = 11
EPS_LN = 1e-5
EPS_DEN = 1e-6
NSLOT = 5
DEBUG = False


class Sem:
    def __init__(self, handle):
        self.h = handle
        self.n = 0


class Sched:

    KEYS = ("t", "v", "s", "g", "y")

    def __init__(self, engine_sems):
        self.q = {k: [] for k in self.KEYS}
        self.esem = engine_sems

    def emit(self, key, fn, waits=(), inc=None, inc_amt=1, dma=False):
        waits = list(waits)
        es = self.esem.get(key)
        if es is not None and not dma:
            waits.append((es, es.n))
            assert inc is None or inc is es
            inc, inc_amt = es, 1
        waits = tuple((s.h, v) for (s, v) in waits if v > 0)
        if inc is not None:
            inc.n += inc_amt
        self.q[key].append((waits, fn, (inc.h if inc is not None else None),
                            inc_amt))

    def run_key(self, key):
        def body(eng):
            if key == "t":
                eng = _SkipGroupCheck(eng)
            for waits, fn, inc_h, amt in self.q[key]:
                for h, v in waits:
                    eng.wait_ge(h, v)
                inst = fn(eng)
                if inc_h is not None:
                    inst.then_inc(inc_h, amt)
        return body


class _SkipGroupCheck:

    def __init__(self, t):
        self._t = t

    def __getattr__(self, k):
        return getattr(self._t, k)

    def matmul(self, *a, **kw):
        kw.setdefault("skip_group_check", True)
        return self._t.matmul(*a, **kw)


def build_program():
    nc = bass.Bass("TRN2", target_bir_lowering=False)

    x_in = nc.dram_tensor("x", [TC, DM], F32, kind="ExternalInput").ap()
    qkv_t = nc.dram_tensor("qkv_t", [48, 128, 2048], F32,
                           kind="ExternalInput").ap()
    wout_t = nc.dram_tensor("wout_t", [16, 128, 2048], F32,
                            kind="ExternalInput").ap()
    gate_t = nc.dram_tensor("gate_t", [128, 128], F32,
                            kind="ExternalInput").ap()
    wg_t = nc.dram_tensor("wg_t", [8, 11, 128, 2048], F32,
                          kind="ExternalInput").ap()
    wu_t = nc.dram_tensor("wu_t", [8, 11, 128, 2048], F32,
                          kind="ExternalInput").ap()
    wd_t = nc.dram_tensor("wd_t", [8, 16, 128, 1408], F32,
                          kind="ExternalInput").ap()
    swg_t = nc.dram_tensor("swg_t", [22, 128, 2048], F32,
                           kind="ExternalInput").ap()
    swu_t = nc.dram_tensor("swu_t", [22, 128, 2048], F32,
                           kind="ExternalInput").ap()
    swd_t = nc.dram_tensor("swd_t", [2, 16, 128, 1408], F32,
                           kind="ExternalInput").ap()
    decay = nc.dram_tensor("decay", [16], F32, kind="ExternalInput").ap()
    percore = nc.dram_tensor("percore", [1, 16], F32, kind="ExternalInput").ap()
    ones_c = nc.dram_tensor("ones_c", [128, 128], F32, kind="ExternalInput").ap()
    ident_c = nc.dram_tensor("ident_c", [128, 128], F32, kind="ExternalInput").ap()
    sele_c = nc.dram_tensor("sele_c", [8, 1024], BF16, kind="ExternalInput").ap()
    out_t = nc.dram_tensor("out", [TC, DM], F32, kind="ExternalOutput").ap()

    if DEBUG:
        dbg_xhat1 = nc.dram_tensor("dbg_xhat1", [128, KC * TC], BF16,
                                   kind="ExternalOutput").ap()
        dbg_outpre = nc.dram_tensor("dbg_outpre", [128, KC * TC], BF16,
                                    kind="ExternalOutput").ap()
        dbg_xres = nc.dram_tensor("dbg_xres", [128, KC * TC], F32,
                                  kind="ExternalOutput").ap()
        dbg_small = nc.dram_tensor("dbg_small", [128, 512], F32,
                                   kind="ExternalOutput").ap()
        dbg_den = nc.dram_tensor("dbg_den", [KC, 2048], F32,
                                 kind="ExternalOutput").ap()

    sp_q = nc.dram_tensor("sp_q", [DM, TC], F32).ap()
    sp_ykv = nc.dram_tensor("sp_ykv", [DM, TC], F32).ap()
    sp_yk = nc.dram_tensor("sp_yk", [DM, TC], F32).ap()
    cc_inA = nc.dram_tensor("cc_inA", [128, 16], F32).ap()
    cc_outA = nc.dram_tensor("cc_outA", [128 * NCORES, 16], F32).ap()
    cc_inB = nc.dram_tensor("cc_inB", [128, 16], F32).ap()
    cc_outB = nc.dram_tensor("cc_outB", [128 * NCORES, 16], F32).ap()
    iota_c = nc.dram_tensor("iota_c", [1, 1024], F32, kind="ExternalInput").ap()

    from contextlib import ExitStack
    with ExitStack() as stack:
        ec = stack.enter_context
        xres = ec(nc.sbuf_tensor("xres", [128, KC * TC], F32))
        xhat = ec(nc.sbuf_tensor("xhat", [128, KC * TC], BF16))
        wbuf = ec(nc.sbuf_tensor("wbuf", [128, NSLOT * 2048], BF16))
        arF = ec(nc.sbuf_tensor("arenaF", [128, 9216], F32))
        gar = ec(nc.sbuf_tensor("garena", [128, HC_E * TC], BF16))
        combbc = ec(nc.sbuf_tensor("combbc", [128, 2 * TC], BF16))
        gwslot = ec(nc.sbuf_tensor("gwslot", [128, 128], BF16))
        onesb = ec(nc.sbuf_tensor("onesb", [128, 128], F32))
        identb = ec(nc.sbuf_tensor("identb", [128, 128], F32))
        seleb = ec(nc.sbuf_tensor("seleb", [8, 1024], BF16))
        combT = ec(nc.sbuf_tensor("combT", [8, 1024], BF16))
        dcol = ec(nc.sbuf_tensor("dcol", [128, 16], F32))
        sbar = ec(nc.sbuf_tensor("sbar", [128, 32], F32))
        gath = ec(nc.sbuf_tensor("gath", [128, 256], F32))
        sin = ec(nc.sbuf_tensor("sin", [128, 32], F32))
        scr32 = ec(nc.sbuf_tensor("scr32", [128, 32], F32))
        scr8 = ec(nc.sbuf_tensor("scr8", [128, 128], F32))
        srow = ec(nc.sbuf_tensor("srow", [1, 160], F32))
        iotarow = ec(nc.sbuf_tensor("iotarow", [1, 1024], F32))
        dprow = ec(nc.sbuf_tensor("dprow", [1, 1024], BF16))
        prow2 = ec(nc.sbuf_tensor("prow2", [1, 2048], F32))
        rbf = ec(nc.sbuf_tensor("rbf", [1, 2048], BF16))
        onesbf = ec(nc.sbuf_tensor("onesbf", [128, 128], BF16))
        pb = [ec(nc.psum_tensor("pb%d" % i, [128, 512], F32))
              for i in range(8)]
        sT = Sem(ec(nc.semaphore("sT")))
        sV = Sem(ec(nc.semaphore("sV")))
        sA = Sem(ec(nc.semaphore("sA")))
        sP = Sem(ec(nc.semaphore("sP")))
        sWs = [Sem(ec(nc.semaphore("sW%d" % i))) for i in range(NSLOT)]
        sXin = [Sem(ec(nc.semaphore("sXin%d" % i))) for i in range(2)]
        sSp = Sem(ec(nc.semaphore("sSp")))
        sIn = [Sem(ec(nc.semaphore("sIn%d" % i))) for i in range(3)]
        sOd = [Sem(ec(nc.semaphore("sOd%d" % i))) for i in range(2)]
        sMisc = Sem(ec(nc.semaphore("sMisc")))
        sCC = Sem(ec(nc.semaphore("sCC")))
        sDbg = Sem(ec(nc.semaphore("sDbg"))) if DEBUG else None
        block = ec(nc.Block())
        print("[kernel build] sbuf bytes remaining:", nc.sbuf_bytes_remaining)

        sch = Sched({"v": sV, "s": sA, "g": sP})
        E = sch.emit

        def fs(lo, hi):
            return arF[:, lo * 1024:hi * 1024]
        S0, S1, S2, S3 = fs(0, 1), fs(1, 2), fs(2, 3), fs(3, 4)
        S4, S5, S6 = fs(4, 5), fs(5, 6), fs(6, 7)
        S7, S8 = fs(7, 8), fs(8, 9)
        mrow = arF[0:1, 0:1024]
        varrow = arF[0:1, 1024:2048]
        arow = arF[0:1, 2048:3072]
        brow = arF[0:1, 3072:4096]
        abc = fs(4, 5)
        bbc = fs(5, 6)
        sqb = [fs(6, 7), fs(7, 8)]
        denrows = [prow2[0:1, 0:1024], prow2[0:1, 1024:2048]]
        rdens = [rbf[0:1, 0:1024], rbf[0:1, 1024:2048]]
        xtm = [arF[:, 0:2048], arF[:, 2048:4096]]
        gsil = fs(0, 1)
        gtmp = fs(1, 2)

        onesrow = onesb[0:1, 0:128]
        onescol = onesb[:, 0:1]
        d_row = srow[0:1, 0:16]
        lnd_row = srow[0:1, 16:32]
        coef2 = srow[0:1, 48:64]
        gv_row = srow[0:1, 80:88]
        sel_row = srow[0:1, 88:96]
        eps_ln_c = srow[0:1, 96:97]
        eps_38_c = srow[0:1, 97:98]
        lnd2 = [srow[0:1, 112:128], srow[0:1, 128:144]]

        bank_free = [None] * 8
        bank_ready = [None] * 8

        def bank_wait(i):
            return [bank_free[i]] if bank_free[i] is not None else []

        wplan = []
        for h in range(KC):
            wplan.append((qkv_t[h], 2048))
            wplan.append((qkv_t[16 + h], 2048))
            wplan.append((qkv_t[32 + h], 2048))
        for ncol in range(16):
            wplan.append((wout_t[ncol], 2048))
        moe_jobs = [dict(g=swg_t, u=swu_t, d=swd_t[0], comb=None, goff=0),
                    dict(g=swg_t, u=swu_t, d=swd_t[1], comb=None, goff=11)]
        for e in range(8):
            moe_jobs.append(dict(g=wg_t[e], u=wu_t[e], d=wd_t[e], comb=e,
                                 goff=0))
        for job in moe_jobs:
            for hc in range(HC_E):
                wplan.append((job["g"][job["goff"] + hc], 2048))
                wplan.append((job["u"][job["goff"] + hc], 2048))
            for ncol in range(16):
                wplan.append((job["d"][ncol], 1408))

        ring = {"issued": 0, "consumed": 0}
        slot_free = [None] * NSLOT
        slot_count = [0] * NSLOT

        def ring_issue():
            i = ring["issued"]
            if i >= len(wplan):
                return
            slot = i % NSLOT
            src, width = wplan[i]
            waits = [slot_free[slot]] if slot_free[slot] is not None else []
            dst = wbuf[:, slot * 2048: slot * 2048 + width]
            E("g", lambda g, s=src, d=dst: g.dma_start(out=d, in_=s),
              waits=waits, inc=sWs[slot], inc_amt=16, dma=True)
            slot_count[slot] += 1
            ring["issued"] = i + 1

        def ring_consume():
            i = ring["consumed"]
            slot = i % NSLOT
            ring["consumed"] = i + 1
            return slot, (sWs[slot], 16 * slot_count[slot])

        def ring_done(slot):
            slot_free[slot] = (sT, sT.n)
            ring_issue()

        def wtile(slot, kc, ncolw=128):
            return wbuf[:, slot * 2048 + kc * ncolw: slot * 2048 + (kc + 1) * ncolw]

        E("g", lambda g: g.dma_start(out=onesb[:, :], in_=ones_c), inc=sMisc,
          inc_amt=16, dma=True)
        E("g", lambda g: g.dma_start(out=identb[:, :], in_=ident_c), inc=sMisc,
          inc_amt=16, dma=True)
        E("g", lambda g: g.dma_start(out=seleb[:, :], in_=sele_c), inc=sMisc,
          inc_amt=16, dma=True)
        E("g", lambda g: g.dma_start(out=d_row, in_=decay.unsqueeze(0)),
          inc=sMisc, inc_amt=16, dma=True)
        E("g", lambda g: g.dma_start(out=srow[0:1, 80:96], in_=percore),
          inc=sMisc, inc_amt=16, dma=True)
        E("g", lambda g: g.dma_start(out=gwslot[:, :], in_=gate_t),
          inc=sMisc, inc_amt=16, dma=True)
        E("g", lambda g: g.dma_start(out=iotarow[:, :], in_=iota_c),
          inc=sMisc, inc_amt=16, dma=True)
        E("g", lambda g: g.dma_start(out=onesbf[:, :], in_=ones_c),
          inc=sMisc, inc_amt=16, dma=True)
        misc_done = (sMisc, sMisc.n)

        E("v", lambda v: v.memset(srow[0:1, 96:97], EPS_LN), waits=[misc_done])
        E("v", lambda v: v.memset(srow[0:1, 97:98], 1e-38))
        E("v", lambda v: v.memset(sin[:, :], 0.0), inc=sV)
        sin_zeroed = (sV, sV.n)

        E("s", lambda s: s.activation(d_row, d_row, AF.Sigmoid),
          waits=[misc_done, sin_zeroed])
        E("s", lambda s: s.activation(lnd_row, d_row, AF.Ln, bias=eps_38_c),
          inc=sA)
        dprep = (sA, sA.n)

        E("t", lambda t: t.matmul(pb[7][:, 0:16], onesrow, d_row, start=True,
                                  stop=True), waits=[dprep], inc=sT)
        E("s", lambda s: s.copy(dcol[:, :], pb[7][:, 0:16]),
          waits=[(sT, sT.n)], inc=sA)
        dcol_done = (sA, sA.n)
        bank_free[7] = dcol_done

        xtm_cons = [0, 0]
        for c in range(8):
            par = c % 2
            waits = [(sT, xtm_cons[par])] if c >= 2 else []
            E("y" if par == 0 else "g",
              lambda y, c=c, par=par: y.dma_start(
                out=xtm[par][:, :],
                in_=x_in.rearrange("(c p) f -> c p f", p=128)[c]),
              waits=waits, inc=sXin[par], inc_amt=16, dma=True)
            xin_wait = (sXin[par], sXin[par].n)
            for b4 in range(4):
                bk = b4
                waits_t = [xin_wait] + bank_wait(bk)
                for i4 in range(4):
                    kc = b4 * 4 + i4
                    E("t", lambda t, par=par, kc=kc, bk=bk, i4=i4:
                      t.transpose(pb[bk][:, i4 * 128:(i4 + 1) * 128],
                                  xtm[par][:, kc * 128:(kc + 1) * 128],
                                  identb[:, :]),
                      waits=(waits_t if i4 == 0 else []),
                      inc=(sT if i4 == 3 else None))
                bank_ready[bk] = (sT, sT.n)
                dst = xres.rearrange("p (kc t) -> p kc t", t=TC)[
                    :, b4 * 4:(b4 + 1) * 4, c * 128:(c + 1) * 128]
                src = pb[bk][:, 0:512].rearrange("p (i j) -> p i j", j=128)
                E("s", lambda s, dst=dst, src=src: s.copy(dst, src),
                  waits=[bank_ready[bk]], inc=sA)
                bank_free[bk] = (sA, sA.n)
            xtm_cons[par] = sT.n
        xin_done = (sA, sA.n)

        for _ in range(NSLOT):
            ring_issue()

        def emit_ln(ready_wait):
            if not isinstance(ready_wait, list):
                ready_wait = [ready_wait] * KC
            stat_banks = {0: 4, 1: 5}
            sq_banks = {0: 0, 1: 1}
            sq_cons = [0, 0]
            sq_keys = ("s", "v", "g")
            for kc in range(KC):
                sq = sqb[kc % 2]
                sq_wait = [ready_wait[kc]]
                if kc >= 2:
                    sq_wait.append((sT, sq_cons[kc % 2]))
                skey = sq_keys[kc % 3]
                if skey == "s":
                    E("s", lambda s, kc=kc, sq=sq: s.square(
                        sq[:, :], xres[:, kc * TC:(kc + 1) * TC]),
                      waits=sq_wait)
                else:
                    E(skey, lambda e, kc=kc, sq=sq: e.tensor_mul(
                        sq[:, :], xres[:, kc * TC:(kc + 1) * TC],
                        xres[:, kc * TC:(kc + 1) * TC]),
                      waits=sq_wait)
                es = sch.esem[skey]
                sq_ready = (es, es.n)
                for th in range(2):
                    bkx = stat_banks[th]
                    w = [ready_wait[kc]] + (bank_wait(bkx) if kc == 0 else [])
                    E("t", lambda t, kc=kc, th=th, bkx=bkx: t.matmul(
                        pb[bkx][0:1, :], onescol,
                        xres[:, kc * TC + th * 512: kc * TC + (th + 1) * 512],
                        start=(kc == 0), stop=(kc == KC - 1)),
                      waits=w, inc=(sT if kc == KC - 1 else None))
                    bkq = sq_banks[th]
                    w = [sq_ready] + (bank_wait(bkq) if kc == 0 else [])
                    E("t", lambda t, th=th, bkq=bkq, sq=sq: t.matmul(
                        pb[bkq][0:1, :], onescol,
                        sq[:, th * 512:(th + 1) * 512],
                        start=(kc == 0), stop=(kc == KC - 1)),
                      waits=w, inc=(sT if (kc == KC - 1 or th == 1) else None))
                sq_cons[kc % 2] = sT.n
            stats_done = (sT, sT.n)
            E("v", lambda v: v.tensor_copy(mrow[:, 0:512], pb[4][0:1, :]),
              waits=[stats_done])
            E("v", lambda v: v.tensor_copy(mrow[:, 512:1024], pb[5][0:1, :]))
            E("v", lambda v: v.tensor_copy(varrow[:, 0:512], pb[0][0:1, :]))
            E("v", lambda v: v.tensor_copy(varrow[:, 512:1024], pb[1][0:1, :]))
            E("v", lambda v: v.tensor_scalar_mul(mrow, mrow, 1.0 / DM))
            E("v", lambda v: v.tensor_scalar_mul(varrow, varrow, 1.0 / DM))
            E("v", lambda v: v.tensor_mul(arow, mrow, mrow))
            E("v", lambda v: v.tensor_sub(varrow, varrow, arow), inc=sV)
            var_done = (sV, sV.n)
            for i in (0, 1, 4, 5):
                bank_free[i] = var_done
            E("s", lambda s: s.activation(arow, varrow, AF.Ln, bias=eps_ln_c),
              waits=[var_done])
            E("s", lambda s: s.activation(arow, arow, AF.Exp, scale=-0.5),
              inc=sA)
            rstd_done = (sA, sA.n)
            E("v", lambda v: v.tensor_mul(brow, mrow, arow), waits=[rstd_done],
              inc=sV)
            rows_done = (sV, sV.n)
            for bk, row, dst in ((2, arow, abc), (3, brow, bbc)):
                for th in range(2):
                    bki = bk + 4 * th
                    E("t", lambda t, bki=bki, row=row, th=th: t.matmul(
                        pb[bki][:, :], onesrow,
                        row[0:1, th * 512:(th + 1) * 512],
                        start=True, stop=True),
                      waits=[rows_done] + bank_wait(bki), inc=sT)
                    bank_ready[bki] = (sT, sT.n)
                    E("s", lambda s, bki=bki, dst=dst, th=th: s.copy(
                        dst[:, th * 512:(th + 1) * 512], pb[bki][:, :]),
                      waits=[bank_ready[bki]], inc=sA)
                    bank_free[bki] = (sA, sA.n)
            bcast_done = (sA, sA.n)
            marks = []
            for kc in range(KC):
                key = "g" if kc % 3 == 2 else "v"
                sem = sP if key == "g" else sV
                scratch = sqb[1] if key == "g" else sqb[0]
                E(key, lambda e, kc=kc, scratch=scratch: e.tensor_mul(
                    scratch[:, :], xres[:, kc * TC:(kc + 1) * TC], abc[:, :]),
                  waits=[bcast_done])
                E(key, lambda e, kc=kc, scratch=scratch: e.tensor_sub(
                    xhat[:, kc * TC:(kc + 1) * TC], scratch[:, :], bbc[:, :]),
                  inc=sem)
                marks.append((sem, sem.n))
            return marks, (sV, sV.n), (sP, sP.n)

        ln1_marks, ln1_v, ln1_p = emit_ln(xin_done)
        if DEBUG:
            E("y", lambda y: y.dma_start(out=dbg_xhat1, in_=xhat[:, :]),
              waits=[ln1_v, ln1_p], inc=sDbg, inc_amt=16)

        qb = [S0, S1]
        kb = [S2, S3]
        vb = [S4, S5]
        ykvb = [S6, sqb[1]]
        qkv_pairs = [(0, 1), (2, 3), (4, 5)]
        head_buf_free = {0: None, 1: None}
        sbar_marks = [None] * KC

        for half in range(2):
            for twoi in range(2):
                E("v", lambda v, half=half, twoi=twoi: v.tensor_copy(
                    lnd2[half].rearrange("o (i two) -> o two i", two=2)[
                        :, twoi, :],
                    lnd_row[0:1, half * 8:(half + 1) * 8]),
                  waits=([dprep] if half == 0 and twoi == 0 else []),
                  inc=(sV if half == 1 and twoi == 1 else None))
        lnd2_done = (sV, sV.n)

        def emit_gather_issue(half, sbar_mark):
            cin = cc_inA if half == 0 else cc_inB
            cout = cc_outA if half == 0 else cc_outB
            E("g", lambda g, cin=cin, half=half: g.dma_start(
                out=cin, in_=sbar[:, half * 16:(half + 1) * 16]),
              waits=[sbar_mark], inc=sMisc, inc_amt=16, dma=True)
            E("g", lambda g, cin=cin, cout=cout: g.collective_compute(
                "AllGather", OP.bypass, replica_groups=[list(range(NCORES))],
                ins=[cin.opt()], outs=[cout.opt()]),
              waits=[(sMisc, sMisc.n)], inc=sCC, inc_amt=1, dma=True)
            return (sCC, sCC.n)

        def emit_gather_load(half, cc_mark, queue="g"):
            cout = cc_outA if half == 0 else cc_outB
            E(queue, lambda g, cout=cout, half=half: g.dma_start(
                out=gath[:, half * 128:(half + 1) * 128].rearrange(
                    "p (r c) -> p r c", c=16),
                in_=cout.rearrange("(r p) c -> p r c", p=128)),
              waits=[cc_mark], inc=sMisc, inc_amt=16, dma=True)
            return (sMisc, sMisc.n)

        def emit_sin(half, gw_done):
            pe_cons = None
            for r in range(NCORES):
                E("s", lambda s, half=half, r=r: s.activation(
                    coef2, lnd2[half], AF.Exp, scale=gv_row[0:1, r:r + 1]),
                  waits=([lnd2_done] if r == 0 else [pe_cons]))
                E("v", lambda v, r=r: v.tensor_scalar_mul(
                    coef2, coef2, sel_row[0:1, r:r + 1]),
                  waits=[(sA, sA.n)], inc=sV)
                cf = (sV, sV.n)
                bk = 6 if r % 2 == 0 else 7
                E("t", lambda t, bk=bk: t.matmul(pb[bk][:, 0:16], onesrow,
                                                 coef2, start=True, stop=True),
                  waits=[cf] + bank_wait(bk), inc=sT)
                pe_cons = (sT, sT.n)
                bank_ready[bk] = pe_cons
                E("v", lambda v, r=r, bk=bk, half=half: v.tensor_mul(
                    scr32[:, 0:16],
                    gath[:, half * 128 + r * 16: half * 128 + (r + 1) * 16],
                    pb[bk][:, 0:16]),
                  waits=[bank_ready[bk], gw_done, sin_zeroed])
                E("v", lambda v, half=half: v.tensor_add(
                    sin[:, half * 16:(half + 1) * 16],
                    sin[:, half * 16:(half + 1) * 16], scr32[:, 0:16]),
                  inc=sV)
                bank_free[bk] = (sV, sV.n)
            return (sV, sV.n)

        gath_done = [None, None]
        sin_done2 = [None, None]

        for h in range(KC):
            par = h % 2
            col_psum = {}
            for ci, role in enumerate(("q", "k", "v")):
                slot, wl_wait = ring_consume()
                pr = qkv_pairs[ci]
                waits = [wl_wait]
                waits += bank_wait(pr[0]) + bank_wait(pr[1])
                for kc in range(KC):
                    kw = list(waits) if kc == 0 else []
                    if h < 2:
                        kw.append(ln1_marks[kc])
                    for th in range(2):
                        E("t", lambda t, kc=kc, th=th, pr=pr, slot=slot:
                          t.matmul(pb[pr[th]][:, :], wtile(slot, kc),
                                   xhat[:, kc * TC + th * 512:
                                        kc * TC + (th + 1) * 512],
                                   start=(kc == 0), stop=(kc == KC - 1)),
                          waits=(kw if th == 0 else []),
                          inc=(sT if kc == KC - 1 and th == 1 else None))
                ring_done(slot)
                col_psum[role] = (pr, (sT, sT.n))
            bufw = [head_buf_free[par]] if head_buf_free[par] else []
            phi_done = {}
            for role, buf in (("q", qb[par]), ("k", kb[par])):
                pr, rdy = col_psum[role]
                exm = []
                for th in range(2):
                    sl = buf[:, th * 512:(th + 1) * 512]
                    E("v", lambda v, pr=pr, th=th, sl=sl:
                      v.tensor_single_scalar(sl, pb[pr[th]][:, :], 0.0, OP.min),
                      waits=([rdy] + bufw if th == 0 else []), inc=sV)
                    mn = (sV, sV.n)
                    E("s", lambda s, sl=sl: s.activation(sl, sl, AF.Exp),
                      waits=[mn], inc=sA)
                    exm.append((sA, sA.n))
                for th in range(2):
                    sl = buf[:, th * 512:(th + 1) * 512]
                    E("v", lambda v, pr=pr, th=th, sl=sl:
                      v.scalar_tensor_tensor(sl, pb[pr[th]][:, :], 0.0, sl,
                                             OP.max, OP.add),
                      waits=[exm[th]], inc=sV)
                phi_done[role] = (sV, sV.n)
                bank_free[pr[0]] = phi_done[role]
                bank_free[pr[1]] = phi_done[role]
            pr, rdy = col_psum["v"]
            for th in range(2):
                E("s", lambda s, pr=pr, th=th, par=par: s.copy(
                    vb[par][:, th * 512:(th + 1) * 512], pb[pr[th]][:, :]),
                  waits=([rdy] + bufw if th == 0 else []), inc=sA)
            v_done = (sA, sA.n)
            bank_free[pr[0]] = v_done
            bank_free[pr[1]] = v_done
            E("v", lambda v, par=par: v.tensor_mul(
                vb[par][:, :], kb[par][:, :], vb[par][:, :]),
              waits=[v_done], inc=sV)
            kv_done = (sV, sV.n)
            scan1_w = [dcol_done] + bufw
            if h < 2:
                scan1_w.append(ln1_p)
            E("v", lambda v, h=h, par=par: v.tensor_tensor_scan(
                ykvb[par][:, :], dcol[:, h:h + 1].broadcast_to((128, TC)),
                vb[par][:, :], 0.0, OP.mult, OP.add), waits=scan1_w)
            E("v", lambda v, h=h, par=par: v.tensor_copy(
                sbar[:, 2 * h:2 * h + 1], ykvb[par][:, 1023:1024]))
            E("v", lambda v, h=h, par=par: v.tensor_tensor_scan(
                vb[par][:, :], dcol[:, h:h + 1].broadcast_to((128, TC)),
                kb[par][:, :], 0.0, OP.mult, OP.add))
            E("v", lambda v, h=h, par=par: v.tensor_copy(
                sbar[:, 2 * h + 1:2 * h + 2], vb[par][:, 1023:1024]), inc=sV)
            sbar_marks[h] = (sV, sV.n)
            E("y", lambda y, h=h, par=par: y.dma_start(
                out=sp_q.rearrange("(hh p) t -> hh p t", p=128)[h],
                in_=qb[par][:, :]), waits=[phi_done["q"]], inc=sSp, inc_amt=16)
            E("y", lambda y, h=h, par=par: y.dma_start(
                out=sp_ykv.rearrange("(hh p) t -> hh p t", p=128)[h],
                in_=ykvb[par][:, :]), waits=[sbar_marks[h]], inc=sSp,
              inc_amt=16)
            E("y", lambda y, h=h, par=par: y.dma_start(
                out=sp_yk.rearrange("(hh p) t -> hh p t", p=128)[h],
                in_=vb[par][:, :]), waits=[], inc=sSp, inc_amt=16)
            head_buf_free[par] = (sSp, sSp.n)
            if h == 7:
                ccA = emit_gather_issue(0, sbar_marks[7])
            if h == 13:
                gath_done[0] = emit_gather_load(0, ccA)
                sin_done2[0] = emit_sin(0, gath_done[0])
        spill_all = (sSp, sSp.n)
        ccB = emit_gather_issue(1, sbar_marks[15])

        p3sets = [(S0, S1, S2), (S3, S4, S5), (S6, S7, S8)]
        p3_slot_free = {0: None, 1: None, 2: None}
        dd_banks = [(0, 1), (2, 3)]
        bc_banks = [(4, 5), (6, 7)]
        p3_state = {"dp_cons": None}
        corr_marks = [None] * KC

        def p3_stage1(h):
            si = h % 3
            bq, by1, by2 = p3sets[si]
            waits_in = [spill_all]
            if p3_slot_free[si]:
                waits_in.append(p3_slot_free[si])
            for src, dst in ((sp_q, bq), (sp_ykv, by1), (sp_yk, by2)):
                E("y", lambda y, src=src, dst=dst, h=h: y.dma_start(
                    out=dst[:, :],
                    in_=src.rearrange("(hh p) t -> hh p t", p=128)[h]),
                  waits=waits_in, inc=sIn[si], inc_amt=16)
                waits_in = []
            in_done = (sIn[si], sIn[si].n)
            E("s", lambda s, h=h: s.activation(
                dprow[0:1, :], iotarow[0:1, :], AF.Exp,
                scale=lnd_row[0:1, h:h + 1]),
              waits=([misc_done, dprep] if p3_state["dp_cons"] is None
                     else [p3_state["dp_cons"]]))
            dp_done = (sA, sA.n)
            dbk = dd_banks[h % 2]
            for th in range(2):
                E("t", lambda t, th=th, dbk=dbk: t.matmul(
                    pb[dbk[th]][:, :], onesbf[0:1, :],
                    dprow[0:1, th * 512:(th + 1) * 512],
                    start=True, stop=True),
                  waits=([dp_done] + bank_wait(dbk[0]) + bank_wait(dbk[1])
                         if th == 0 else []),
                  inc=(sT if th == 1 else None))
            dpmm = (sT, sT.n)
            p3_state["dp_cons"] = dpmm
            sin_w = sin_done2[0] if h < 8 else sin_done2[1]
            first = [dpmm, in_done, sin_w]
            for ci, ybuf in ((0, by1), (1, by2)):
                for th in range(2):
                    E("v", lambda v, ci=ci, ybuf=ybuf, th=th, h=h, dbk=dbk:
                      v.scalar_tensor_tensor(
                        ybuf[:, th * 512:(th + 1) * 512], pb[dbk[th]][:, :],
                        sin[:, 2 * h + ci:2 * h + ci + 1],
                        ybuf[:, th * 512:(th + 1) * 512], OP.mult, OP.add),
                      waits=first,
                      inc=(sV if ci == 1 and th == 1 else None))
                    first = []
            corr_marks[h] = (sV, sV.n)

        def p3_stage2(h):
            si = h % 3
            bq, by1, by2 = p3sets[si]
            dbk = dd_banks[h % 2]
            bcb = bc_banks[h % 2]
            dnr = denrows[h % 2]
            rdn = rdens[h % 2]
            prodsl = xhat[:, h * TC:(h + 1) * TC]
            prod_th = []
            for th in range(2):
                E("g", lambda g, bq=bq, by2=by2, prodsl=prodsl, th=th:
                  g.tensor_mul(
                    prodsl[:, th * 512:(th + 1) * 512],
                    bq[:, th * 512:(th + 1) * 512],
                    by2[:, th * 512:(th + 1) * 512]),
                  waits=([corr_marks[h]] if th == 0 else []), inc=sP)
                prod_th.append((sP, sP.n))
            for th in range(2):
                E("t", lambda t, th=th, dbk=dbk, prodsl=prodsl: t.matmul(
                    pb[dbk[th]][0:1, :], onesbf[:, 0:1],
                    prodsl[:, th * 512:(th + 1) * 512],
                    start=True, stop=True),
                  waits=[prod_th[th]],
                  inc=(sT if th == 1 else None))
            den_mm = (sT, sT.n)
            for th in range(2):
                E("v", lambda v, th=th, dbk=dbk, dnr=dnr:
                  v.tensor_single_scalar(
                    dnr[0:1, th * 512:(th + 1) * 512], pb[dbk[th]][0:1, :],
                    EPS_DEN, OP.max),
                  waits=([den_mm] if th == 0 else []),
                  inc=(sV if th == 1 else None))
            den_row_done = (sV, sV.n)
            bank_free[dbk[0]] = den_row_done
            bank_free[dbk[1]] = den_row_done
            E("s", lambda s, dnr=dnr: s.activation(dnr, dnr, AF.Ln),
              waits=[den_row_done])
            E("s", lambda s, rdn=rdn, dnr=dnr: s.activation(
                rdn, dnr, AF.Exp, scale=-1.0), inc=sA)
            rden_done = (sA, sA.n)
            for th in range(2):
                E("t", lambda t, th=th, bcb=bcb, rdn=rdn: t.matmul(
                    pb[bcb[th]][:, :], onesbf[0:1, :],
                    rdn[0:1, th * 512:(th + 1) * 512], start=True, stop=True),
                  waits=([rden_done] + bank_wait(bcb[0]) + bank_wait(bcb[1])
                         if th == 0 else []),
                  inc=(sT if th == 1 else None))
            bc_done = (sT, sT.n)
            E("g", lambda g, bq=bq, by1=by1: g.tensor_mul(
                bq[:, :], bq[:, :], by1[:, :]),
              waits=[corr_marks[h]], inc=sP)
            qm_done = (sP, sP.n)
            for th in range(2):
                E("v", lambda v, th=th, h=h, bq=bq, bcb=bcb: v.tensor_mul(
                    xhat[:, h * TC + th * 512: h * TC + (th + 1) * 512],
                    bq[:, th * 512:(th + 1) * 512], pb[bcb[th]][:, :]),
                  waits=([bc_done, qm_done] if th == 0 else []),
                  inc=(sV if th == 1 else None))
            outpre_done = (sV, sV.n)
            bank_free[bcb[0]] = outpre_done
            bank_free[bcb[1]] = outpre_done
            p3_slot_free[si] = outpre_done

        for h in range(KC):
            if h == 7:
                gath_done[1] = emit_gather_load(1, ccB, queue="y")
            if h == 8:
                p3_stage2(6)
                sin_done2[1] = emit_sin(1, gath_done[1])
            elif h >= 2:
                p3_stage2(h - 2)
            p3_stage1(h)
        p3_stage2(KC - 2)
        p3_stage2(KC - 1)
        outpre_all = (sV, sV.n)
        if DEBUG:
            E("y", lambda y: y.dma_start(out=dbg_outpre, in_=xhat[:, :]),
              waits=[outpre_all], inc=sDbg, inc_amt=16)
            E("y", lambda y: y.dma_start(out=dbg_small[:, 0:32],
                                         in_=sbar[:, :]),
              waits=[sbar_marks[15]], inc=sDbg, inc_amt=16)
            E("y", lambda y: y.dma_start(out=dbg_small[:, 32:64],
                                         in_=sin[:, :]),
              waits=[sin_done2[1]], inc=sDbg, inc_amt=16)
            E("y", lambda y: y.dma_start(out=dbg_small[:, 64:320],
                                         in_=gath[:, :]),
              waits=[gath_done[1]], inc=sDbg, inc_amt=16)
            E("y", lambda y: y.dma_start(out=dbg_small[:, 320:336],
                                         in_=dcol[:, :]),
              waits=[dcol_done], inc=sDbg, inc_amt=16)
            E("y", lambda y: y.dma_start(out=dbg_small[0:1, 336:464],
                                         in_=srow[0:1, 0:128]),
              waits=[sin_done2[1]], inc=sDbg, inc_amt=16)

        op_pairs = [(0, 1), (2, 3)]
        xres_marks = []
        for ncol in range(16):
            slot, wl_wait = ring_consume()
            pr = op_pairs[ncol % 2]
            waits = [wl_wait, outpre_all] + bank_wait(pr[0]) + bank_wait(pr[1])
            for kc in range(KC):
                for th in range(2):
                    E("t", lambda t, kc=kc, th=th, pr=pr, slot=slot: t.matmul(
                        pb[pr[th]][:, :], wtile(slot, kc),
                        xhat[:, kc * TC + th * 512: kc * TC + (th + 1) * 512],
                        start=(kc == 0), stop=(kc == KC - 1)),
                      waits=(waits if kc == 0 and th == 0 else []),
                      inc=(sT if kc == KC - 1 and th == 1 else None))
            ring_done(slot)
            rdy = (sT, sT.n)
            for th in range(2):
                sl = xres[:, ncol * TC + th * 512: ncol * TC + (th + 1) * 512]
                E("v", lambda v, sl=sl, pr=pr, th=th: v.tensor_add(
                    sl, sl, pb[pr[th]][:, :]),
                  waits=([rdy] if th == 0 else []),
                  inc=(sV if th == 1 else None))
            ev = (sV, sV.n)
            bank_free[pr[0]] = ev
            bank_free[pr[1]] = ev
            xres_marks.append(ev)
        attn_done = (sV, sV.n)
        if DEBUG:
            E("y", lambda y: y.dma_start(out=dbg_xres, in_=xres[:, :]),
              waits=[attn_done], inc=sDbg, inc_amt=16)

        ln2_marks, ln2_v, ln2_p = emit_ln(xres_marks)

        mx = scr8[:, 0:1]
        mxn = scr8[:, 1:2]
        sume = scr8[:, 2:3]
        rsum = scr8[:, 3:4]
        escr = scr8[:, 8:16]
        cnt = scr8[:, 16:24]
        tmp8 = scr8[:, 24:32]
        comb_marks = [None] * 8
        for tc in range(8):
            bk = 4 + tc % 4
            scomb = scr8[:, 64 + tc * 8: 72 + tc * 8]
            waits = [misc_done] + bank_wait(bk)
            for kc in range(KC):
                w = []
                if tc < 2:
                    w.append(ln2_marks[kc])
                if kc == 0:
                    w = w + waits
                E("t", lambda t, kc=kc, tc=tc, bk=bk: t.matmul(
                    pb[bk][:, 0:8],
                    xhat[:, kc * TC + tc * 128: kc * TC + (tc + 1) * 128],
                    gwslot[:, kc * 8:(kc + 1) * 8],
                    start=(kc == 0), stop=(kc == KC - 1)),
                  waits=w, inc=(sT if kc == KC - 1 else None))
            lg = (sT, sT.n)
            E("v", lambda v, bk=bk: v.reduce_max(mx, pb[bk][:, 0:8], axis=AX),
              waits=[lg])
            E("v", lambda v: v.tensor_scalar_mul(mxn, mx, -1.0), inc=sV)
            E("s", lambda s, bk=bk: s.activation(escr, pb[bk][:, 0:8], AF.Exp,
                                                 bias=mxn),
              waits=[(sV, sV.n)], inc=sA)
            ex_done = (sA, sA.n)
            bank_free[bk] = ex_done
            E("v", lambda v: v.reduce_sum(sume, escr, axis=AX), waits=[ex_done])
            E("v", lambda v: v.reciprocal(rsum, sume))
            E("v", lambda v: v.tensor_scalar_mul(escr, escr, rsum))
            E("v", lambda v: v.memset(cnt, 0.0))
            for j in range(8):
                E("v", lambda v, j=j: v.tensor_single_scalar(
                    tmp8, escr, escr[:, j:j + 1], OP.is_lt))
                E("v", lambda v: v.tensor_add(cnt, cnt, tmp8))
            E("v", lambda v: v.tensor_single_scalar(tmp8, cnt, 1.5, OP.is_lt))
            E("v", lambda v, scomb=scomb: v.tensor_mul(scomb, escr, tmp8),
              inc=sV)
            comb_marks[tc] = (sV, sV.n)

        def emit_combT():
            for tc in range(8):
                scomb = scr8[:, 64 + tc * 8: 72 + tc * 8]
                bkt = 6 if tc % 2 == 0 else 7
                E("t", lambda t, bkt=bkt, scomb=scomb: t.transpose(
                    pb[bkt][0:8, 0:128], scomb, identb[:, :]),
                  waits=[comb_marks[tc]] + bank_wait(bkt), inc=sT)
                tr_done = (sT, sT.n)
                E("s", lambda s, tc=tc, bkt=bkt: s.copy(
                    combT[:, tc * 128:(tc + 1) * 128], pb[bkt][0:8, 0:128]),
                  waits=[tr_done], inc=sA)
                bank_free[bkt] = (sA, sA.n)
            return (sA, sA.n)

        gu_banks = (0, 1, 2, 3)
        down_pairs = [(4, 5), (6, 7)]
        prev_gmark = None
        combT_done = None
        for ji, job in enumerate(moe_jobs):
            if ji == 1:
                combT_done = emit_combT()
            combi = job["comb"]
            cslot = ji % 2
            if combi is not None:
                for th in range(2):
                    bk = gu_banks[2 + th]
                    E("t", lambda t, combi=combi, th=th, bk=bk: t.matmul(
                        pb[bk][:, :], seleb[:, combi * 128:(combi + 1) * 128],
                        combT[:, th * 512:(th + 1) * 512],
                        start=True, stop=True),
                      waits=[combT_done] + bank_wait(bk), inc=sT)
                    bank_ready[bk] = (sT, sT.n)
                    E("s", lambda s, th=th, bk=bk, cslot=cslot: s.copy(
                        combbc[:, cslot * TC + th * 512:
                               cslot * TC + (th + 1) * 512],
                        pb[bk][:, :]),
                      waits=[bank_ready[bk]], inc=sA)
                    bank_free[bk] = (sA, sA.n)
            combbc_done = (sA, sA.n)
            g_all = None
            ev_prev = None
            for hc in range(HC_E):
                gub = (0, 1, 2, 3) if hc % 2 == 0 else (4, 5, 6, 7)
                sg, wg_wait = ring_consume()
                su, wu_wait = ring_consume()
                waits0 = [wg_wait, wu_wait]
                if ji > 0:
                    waits0.append(ln2_v)
                    waits0.append(ln2_p)
                for b in gub:
                    waits0 += bank_wait(b)
                for wi, (slot, b0) in enumerate(((sg, 0), (su, 2))):
                    for th in range(2):
                        bk = gub[b0 + th]
                        for kc in range(KC):
                            kw = (list(waits0) if wi == 0 and th == 0
                                  and kc == 0 else [])
                            if ji == 0 and hc == 0 and wi == 0 and th == 0:
                                kw.append(ln2_marks[kc])
                            E("t", lambda t, kc=kc, bk=bk, th=th, slot=slot:
                              t.matmul(pb[bk][:, :], wtile(slot, kc),
                                       xhat[:, kc * TC + th * 512:
                                            kc * TC + (th + 1) * 512],
                                       start=(kc == 0), stop=(kc == KC - 1)),
                              waits=kw,
                              inc=(sT if kc == KC - 1 else None))
                ring_done(sg)
                ring_done(su)
                gu_rdy = (sT, sT.n)
                gw = [gu_rdy]
                if ev_prev is not None:
                    gw.append(ev_prev)
                if prev_gmark is not None:
                    gw.append(prev_gmark)
                if combi is not None:
                    gw.append(combbc_done)
                for th in range(2):
                    E("s", lambda s, th=th, gub=gub: s.activation(
                        gsil[:, th * 512:(th + 1) * 512],
                        pb[gub[th]][:, :], AF.Silu),
                      waits=(gw if th == 0 else []), inc=sA)
                    sil = (sA, sA.n)
                    gdst = gar[:, hc * TC + th * 512: hc * TC + (th + 1) * 512]
                    if combi is not None:
                        E("v", lambda v, th=th, gub=gub: v.scalar_tensor_tensor(
                            gtmp[:, th * 512:(th + 1) * 512],
                            pb[gub[2 + th]][:, :], 0.0,
                            gsil[:, th * 512:(th + 1) * 512],
                            OP.add, OP.mult),
                          waits=[sil])
                        E("v", lambda v, th=th, gdst=gdst, cslot=cslot:
                          v.tensor_mul(
                            gdst, gtmp[:, th * 512:(th + 1) * 512],
                            combbc[:, cslot * TC + th * 512:
                                   cslot * TC + (th + 1) * 512]),
                          inc=sV)
                    else:
                        E("v", lambda v, th=th, gdst=gdst, gub=gub:
                          v.scalar_tensor_tensor(
                            gdst, pb[gub[2 + th]][:, :], 0.0,
                            gsil[:, th * 512:(th + 1) * 512],
                            OP.add, OP.mult),
                          waits=[sil], inc=sV)
                ev = (sV, sV.n)
                ev_prev = ev
                g_all = ev
                for b in gub:
                    bank_free[b] = ev
            for ncol in range(16):
                slot, wd_wait = ring_consume()
                pr = down_pairs[ncol % 2]
                waits = [wd_wait, g_all] + bank_wait(pr[0]) + bank_wait(pr[1])
                for kc in range(HC_E):
                    for th in range(2):
                        E("t", lambda t, kc=kc, th=th, pr=pr, slot=slot:
                          t.matmul(pb[pr[th]][:, :], wtile(slot, kc),
                                   gar[:, kc * TC + th * 512:
                                       kc * TC + (th + 1) * 512],
                                   start=(kc == 0), stop=(kc == HC_E - 1)),
                          waits=(waits if kc == 0 and th == 0 else []),
                          inc=(sT if kc == HC_E - 1 and th == 1 else None))
                ring_done(slot)
                rdy = (sT, sT.n)
                for th in range(2):
                    sl = xres[:, ncol * TC + th * 512:
                              ncol * TC + (th + 1) * 512]
                    E("v", lambda v, sl=sl, pr=pr, th=th: v.tensor_add(
                        sl, sl, pb[pr[th]][:, :]),
                      waits=([rdy] if th == 0 else []),
                      inc=(sV if th == 1 else None))
                ev = (sV, sV.n)
                bank_free[pr[0]] = ev
                bank_free[pr[1]] = ev
            prev_gmark = (sT, sT.n)
        moe_done = (sV, sV.n)

        for c in range(8):
            par = c % 2
            for b4 in range(4):
                bk = b4
                waits_t = [moe_done] + bank_wait(bk)
                if c >= 2 and b4 == 0:
                    waits_t.append((sOd[par], 16 * (c // 2)))
                for i4 in range(4):
                    kc = b4 * 4 + i4
                    E("t", lambda t, kc=kc, c=c, bk=bk, i4=i4: t.transpose(
                        pb[bk][:, i4 * 128:(i4 + 1) * 128],
                        xres[:, kc * TC + c * 128: kc * TC + (c + 1) * 128],
                        identb[:, :]),
                      waits=(waits_t if i4 == 0 else []),
                      inc=(sT if i4 == 3 else None))
                bank_ready[bk] = (sT, sT.n)
                dst = xtm[par].rearrange("p (kk j) -> p kk j", j=128)[
                    :, b4 * 4:(b4 + 1) * 4, :]
                src = pb[bk][:, 0:512].rearrange("p (i j) -> p i j", j=128)
                E("s", lambda s, dst=dst, src=src: s.copy(dst, src),
                  waits=[bank_ready[bk]], inc=sA)
                bank_free[bk] = (sA, sA.n)
            ev = (sA, sA.n)
            E("y", lambda y, c=c, par=par: y.dma_start(
                out=out_t.rearrange("(c p) f -> c p f", p=128)[c],
                in_=xtm[par][:, :]), waits=[ev], inc=sOd[par], inc_amt=16)
        final_waits = [(sOd[0], sOd[0].n), (sOd[1], sOd[1].n)]
        if DEBUG:
            final_waits.append((sDbg, sDbg.n))
        E("y", lambda y: None, waits=final_waits)

        assert ring["issued"] == len(wplan) and ring["consumed"] == len(wplan)

        block.tensor(sch.run_key("t"))
        block.vector(sch.run_key("v"))
        block.scalar(sch.run_key("s"))
        block.gpsimd(sch.run_key("g"))
        block.sync(sch.run_key("y"))

    return nc


_PROG_CACHE = {}


def _get_program():
    if "nc" not in _PROG_CACHE:
        _PROG_CACHE["nc"] = build_program()
    return _PROG_CACHE["nc"]


def tile_km(W, nkc, ncols):
    return np.ascontiguousarray(
        W.reshape(nkc, 128, ncols, 128).transpose(2, 1, 0, 3)
        .reshape(ncols, 128, nkc * 128))


def make_in_maps(L0_attn_x, L0_attn_w_qkv, L0_attn_w_out, L0_attn_decay_param,
                 L0_ffn_gate_w, L0_ffn_w_gate, L0_ffn_w_up, L0_ffn_w_down,
                 L0_ffn_sw_gate, L0_ffn_sw_up, L0_ffn_sw_down):
    x = np.ascontiguousarray(np.asarray(L0_attn_x, np.float32).reshape(8192, DM))
    dec = np.ascontiguousarray(np.asarray(L0_attn_decay_param, np.float32))
    qkv_t = tile_km(np.asarray(L0_attn_w_qkv, np.float32), 16, 48)
    wout_t = tile_km(np.asarray(L0_attn_w_out, np.float32), 16, 16)
    gw = np.asarray(L0_ffn_gate_w, np.float32)
    gate_t = np.ascontiguousarray(
        gw.reshape(16, 128, 8).transpose(1, 0, 2).reshape(128, 128))
    wg = np.asarray(L0_ffn_w_gate, np.float32)
    wu = np.asarray(L0_ffn_w_up, np.float32)
    wd = np.asarray(L0_ffn_w_down, np.float32)
    wg_t = np.stack([tile_km(wg[e], 16, 11) for e in range(8)])
    wu_t = np.stack([tile_km(wu[e], 16, 11) for e in range(8)])
    wd_t = np.stack([tile_km(wd[e], 11, 16) for e in range(8)])
    swg_t = tile_km(np.asarray(L0_ffn_sw_gate, np.float32), 16, 22)
    swu_t = tile_km(np.asarray(L0_ffn_sw_up, np.float32), 16, 22)
    swd = np.asarray(L0_ffn_sw_down, np.float32)
    swd_t = np.stack([tile_km(swd[0:1408], 11, 16),
                      tile_km(swd[1408:2816], 11, 16)])

    ones_c = np.ones((128, 128), np.float32)
    ident_c = np.eye(128, dtype=np.float32)
    iota = np.arange(1, 1025, dtype=np.float32).reshape(1, 1024)
    sele = np.zeros((8, 1024), np.float32)
    for e in range(8):
        sele[e, e * 128:(e + 1) * 128] = 1.0
    sele = sele.astype(ml_dtypes.bfloat16)

    in_maps = []
    for c in range(NCORES):
        batch0 = (c // 4) * 4
        pc = np.zeros((1, 16), np.float32)
        for g, cp in enumerate(range(c - 1, batch0 - 1, -1)):
            pc[0, cp] = 1024.0 * g
            pc[0, 8 + cp] = 1.0
        in_maps.append({
            "x": x[c * TC:(c + 1) * TC],
            "qkv_t": qkv_t, "wout_t": wout_t, "gate_t": gate_t,
            "wg_t": wg_t, "wu_t": wu_t, "wd_t": wd_t,
            "swg_t": swg_t, "swu_t": swu_t, "swd_t": swd_t,
            "decay": dec, "percore": pc,
            "ones_c": ones_c, "ident_c": ident_c, "sele_c": sele,
            "iota_c": iota,
        })
    return in_maps


def kernel(**inputs):
    in_maps = make_in_maps(**inputs)
    nc = _get_program()
    res = run_bass_kernel_spmd(nc, in_maps, core_ids=list(range(NCORES)))
    out = np.concatenate([res.results[c]["out"] for c in range(NCORES)], axis=0)
    return out.reshape(2, 4096, DM).astype(np.float32)


# version 31 (final)
# speedup vs baseline: 1.0161x; 1.0029x over previous
import numpy as np
import ml_dtypes
import concourse.bass as bass
import concourse.mybir as mybir
from concourse.bass_utils import run_bass_kernel_spmd

F32 = mybir.dt.float32
BF16 = mybir.dt.bfloat16
AF = mybir.ActivationFunctionType
OP = mybir.AluOpType
AX = mybir.AxisListType.X

NCORES = 8
TC = 1024
KC = 16
DM = 2048
HC_E = 11
EPS_LN = 1e-5
EPS_DEN = 1e-6
NSLOT = 5
DEBUG = False


class Sem:
    def __init__(self, handle):
        self.h = handle
        self.n = 0


class Sched:

    KEYS = ("t", "v", "s", "g", "y")

    def __init__(self, engine_sems):
        self.q = {k: [] for k in self.KEYS}
        self.esem = engine_sems

    def emit(self, key, fn, waits=(), inc=None, inc_amt=1, dma=False):
        waits = list(waits)
        es = self.esem.get(key)
        if es is not None and not dma:
            waits.append((es, es.n))
            assert inc is None or inc is es
            inc, inc_amt = es, 1
        waits = tuple((s.h, v) for (s, v) in waits if v > 0)
        if inc is not None:
            inc.n += inc_amt
        self.q[key].append((waits, fn, (inc.h if inc is not None else None),
                            inc_amt))

    def run_key(self, key):
        def body(eng):
            if key == "t":
                eng = _SkipGroupCheck(eng)
            for waits, fn, inc_h, amt in self.q[key]:
                for h, v in waits:
                    eng.wait_ge(h, v)
                inst = fn(eng)
                if inc_h is not None:
                    inst.then_inc(inc_h, amt)
        return body


class _SkipGroupCheck:

    def __init__(self, t):
        self._t = t

    def __getattr__(self, k):
        return getattr(self._t, k)

    def matmul(self, *a, **kw):
        kw.setdefault("skip_group_check", True)
        return self._t.matmul(*a, **kw)


def build_program():
    nc = bass.Bass("TRN2", target_bir_lowering=False)

    x_in = nc.dram_tensor("x", [TC, DM], F32, kind="ExternalInput").ap()
    qkv_t = nc.dram_tensor("qkv_t", [48, 128, 2048], F32,
                           kind="ExternalInput").ap()
    wout_t = nc.dram_tensor("wout_t", [16, 128, 2048], F32,
                            kind="ExternalInput").ap()
    gate_t = nc.dram_tensor("gate_t", [128, 128], F32,
                            kind="ExternalInput").ap()
    wg_t = nc.dram_tensor("wg_t", [8, 11, 128, 2048], F32,
                          kind="ExternalInput").ap()
    wu_t = nc.dram_tensor("wu_t", [8, 11, 128, 2048], F32,
                          kind="ExternalInput").ap()
    wd_t = nc.dram_tensor("wd_t", [8, 16, 128, 1408], F32,
                          kind="ExternalInput").ap()
    swg_t = nc.dram_tensor("swg_t", [22, 128, 2048], F32,
                           kind="ExternalInput").ap()
    swu_t = nc.dram_tensor("swu_t", [22, 128, 2048], F32,
                           kind="ExternalInput").ap()
    swd_t = nc.dram_tensor("swd_t", [2, 16, 128, 1408], F32,
                           kind="ExternalInput").ap()
    decay = nc.dram_tensor("decay", [16], F32, kind="ExternalInput").ap()
    percore = nc.dram_tensor("percore", [1, 16], F32, kind="ExternalInput").ap()
    ones_c = nc.dram_tensor("ones_c", [128, 128], F32, kind="ExternalInput").ap()
    ident_c = nc.dram_tensor("ident_c", [128, 128], F32, kind="ExternalInput").ap()
    sele_c = nc.dram_tensor("sele_c", [8, 1024], BF16, kind="ExternalInput").ap()
    out_t = nc.dram_tensor("out", [TC, DM], F32, kind="ExternalOutput").ap()

    if DEBUG:
        dbg_xhat1 = nc.dram_tensor("dbg_xhat1", [128, KC * TC], BF16,
                                   kind="ExternalOutput").ap()
        dbg_outpre = nc.dram_tensor("dbg_outpre", [128, KC * TC], BF16,
                                    kind="ExternalOutput").ap()
        dbg_xres = nc.dram_tensor("dbg_xres", [128, KC * TC], F32,
                                  kind="ExternalOutput").ap()
        dbg_small = nc.dram_tensor("dbg_small", [128, 512], F32,
                                   kind="ExternalOutput").ap()
        dbg_den = nc.dram_tensor("dbg_den", [KC, 2048], F32,
                                 kind="ExternalOutput").ap()

    sp_q = nc.dram_tensor("sp_q", [DM, TC], F32).ap()
    sp_ykv = nc.dram_tensor("sp_ykv", [DM, TC], F32).ap()
    sp_yk = nc.dram_tensor("sp_yk", [DM, TC], F32).ap()
    cc_inA = nc.dram_tensor("cc_inA", [128, 16], F32).ap()
    cc_outA = nc.dram_tensor("cc_outA", [128 * NCORES, 16], F32).ap()
    cc_inB = nc.dram_tensor("cc_inB", [128, 16], F32).ap()
    cc_outB = nc.dram_tensor("cc_outB", [128 * NCORES, 16], F32).ap()
    iota_c = nc.dram_tensor("iota_c", [1, 1024], F32, kind="ExternalInput").ap()

    from contextlib import ExitStack
    with ExitStack() as stack:
        ec = stack.enter_context
        xres = ec(nc.sbuf_tensor("xres", [128, KC * TC], F32))
        xhat = ec(nc.sbuf_tensor("xhat", [128, KC * TC], BF16))
        wbuf = ec(nc.sbuf_tensor("wbuf", [128, NSLOT * 2048], BF16))
        arF = ec(nc.sbuf_tensor("arenaF", [128, 9216], F32))
        gar = ec(nc.sbuf_tensor("garena", [128, HC_E * TC], BF16))
        combbc = ec(nc.sbuf_tensor("combbc", [128, 2 * TC], BF16))
        gwslot = ec(nc.sbuf_tensor("gwslot", [128, 128], BF16))
        onesb = ec(nc.sbuf_tensor("onesb", [128, 128], F32))
        identb = ec(nc.sbuf_tensor("identb", [128, 128], F32))
        seleb = ec(nc.sbuf_tensor("seleb", [8, 1024], BF16))
        combT = ec(nc.sbuf_tensor("combT", [8, 1024], BF16))
        dcol = ec(nc.sbuf_tensor("dcol", [128, 16], F32))
        sbar = ec(nc.sbuf_tensor("sbar", [128, 32], F32))
        gath = ec(nc.sbuf_tensor("gath", [128, 256], F32))
        sin = ec(nc.sbuf_tensor("sin", [128, 32], F32))
        scr32 = ec(nc.sbuf_tensor("scr32", [128, 32], F32))
        scr8 = ec(nc.sbuf_tensor("scr8", [128, 128], F32))
        coefall = ec(nc.sbuf_tensor("coefall", [1, 128], F32))
        srow = ec(nc.sbuf_tensor("srow", [1, 160], F32))
        iotarow = ec(nc.sbuf_tensor("iotarow", [1, 1024], F32))
        dprow = ec(nc.sbuf_tensor("dprow", [1, 1024], BF16))
        prow2 = ec(nc.sbuf_tensor("prow2", [1, 2048], F32))
        rbf = ec(nc.sbuf_tensor("rbf", [1, 2048], BF16))
        onesbf = ec(nc.sbuf_tensor("onesbf", [128, 128], BF16))
        pb = [ec(nc.psum_tensor("pb%d" % i, [128, 512], F32))
              for i in range(8)]
        sT = Sem(ec(nc.semaphore("sT")))
        sV = Sem(ec(nc.semaphore("sV")))
        sA = Sem(ec(nc.semaphore("sA")))
        sP = Sem(ec(nc.semaphore("sP")))
        sWs = [Sem(ec(nc.semaphore("sW%d" % i))) for i in range(NSLOT)]
        sXin = [Sem(ec(nc.semaphore("sXin%d" % i))) for i in range(2)]
        sSp = Sem(ec(nc.semaphore("sSp")))
        sIn = [Sem(ec(nc.semaphore("sIn%d" % i))) for i in range(3)]
        sOd = [Sem(ec(nc.semaphore("sOd%d" % i))) for i in range(2)]
        sMisc = Sem(ec(nc.semaphore("sMisc")))
        sCC = Sem(ec(nc.semaphore("sCC")))
        sDbg = Sem(ec(nc.semaphore("sDbg"))) if DEBUG else None
        block = ec(nc.Block())
        print("[kernel build] sbuf bytes remaining:", nc.sbuf_bytes_remaining)

        sch = Sched({"v": sV, "s": sA, "g": sP})
        E = sch.emit

        def fs(lo, hi):
            return arF[:, lo * 1024:hi * 1024]
        S0, S1, S2, S3 = fs(0, 1), fs(1, 2), fs(2, 3), fs(3, 4)
        S4, S5, S6 = fs(4, 5), fs(5, 6), fs(6, 7)
        S7, S8 = fs(7, 8), fs(8, 9)
        mrow = arF[0:1, 0:1024]
        varrow = arF[0:1, 1024:2048]
        arow = arF[0:1, 2048:3072]
        brow = arF[0:1, 3072:4096]
        abc = fs(4, 5)
        bbc = fs(5, 6)
        sqb = [fs(6, 7), fs(7, 8)]
        denrows = [prow2[0:1, 0:1024], prow2[0:1, 1024:2048]]
        rdens = [rbf[0:1, 0:1024], rbf[0:1, 1024:2048]]
        xtm = [arF[:, 0:2048], arF[:, 2048:4096]]
        gsil = fs(0, 1)
        gtmp = fs(1, 2)

        onesrow = onesb[0:1, 0:128]
        onescol = onesb[:, 0:1]
        d_row = srow[0:1, 0:16]
        lnd_row = srow[0:1, 16:32]
        coef2 = srow[0:1, 48:64]
        gv_row = srow[0:1, 80:88]
        sel_row = srow[0:1, 88:96]
        eps_ln_c = srow[0:1, 96:97]
        eps_38_c = srow[0:1, 97:98]
        lnd2 = [srow[0:1, 112:128], srow[0:1, 128:144]]

        bank_free = [None] * 8
        bank_ready = [None] * 8

        def bank_wait(i):
            return [bank_free[i]] if bank_free[i] is not None else []

        wplan = []
        for h in range(KC):
            wplan.append((qkv_t[h], 2048))
            wplan.append((qkv_t[16 + h], 2048))
            wplan.append((qkv_t[32 + h], 2048))
        for ncol in range(16):
            wplan.append((wout_t[ncol], 2048))
        moe_jobs = [dict(g=swg_t, u=swu_t, d=swd_t[0], comb=None, goff=0),
                    dict(g=swg_t, u=swu_t, d=swd_t[1], comb=None, goff=11)]
        for e in range(8):
            moe_jobs.append(dict(g=wg_t[e], u=wu_t[e], d=wd_t[e], comb=e,
                                 goff=0))
        for job in moe_jobs:
            for hc in range(HC_E):
                wplan.append((job["g"][job["goff"] + hc], 2048))
                wplan.append((job["u"][job["goff"] + hc], 2048))
            for ncol in range(16):
                wplan.append((job["d"][ncol], 1408))

        ring = {"issued": 0, "consumed": 0}
        slot_free = [None] * NSLOT
        slot_count = [0] * NSLOT

        def ring_issue():
            i = ring["issued"]
            if i >= len(wplan):
                return
            slot = i % NSLOT
            src, width = wplan[i]
            waits = [slot_free[slot]] if slot_free[slot] is not None else []
            dst = wbuf[:, slot * 2048: slot * 2048 + width]
            E("g", lambda g, s=src, d=dst: g.dma_start(out=d, in_=s),
              waits=waits, inc=sWs[slot], inc_amt=16, dma=True)
            slot_count[slot] += 1
            ring["issued"] = i + 1

        def ring_consume():
            i = ring["consumed"]
            slot = i % NSLOT
            ring["consumed"] = i + 1
            return slot, (sWs[slot], 16 * slot_count[slot])

        def ring_done(slot):
            slot_free[slot] = (sT, sT.n)
            ring_issue()

        def wtile(slot, kc, ncolw=128):
            return wbuf[:, slot * 2048 + kc * ncolw: slot * 2048 + (kc + 1) * ncolw]

        E("g", lambda g: g.dma_start(out=onesb[:, :], in_=ones_c), inc=sMisc,
          inc_amt=16, dma=True)
        E("g", lambda g: g.dma_start(out=identb[:, :], in_=ident_c), inc=sMisc,
          inc_amt=16, dma=True)
        E("g", lambda g: g.dma_start(out=seleb[:, :], in_=sele_c), inc=sMisc,
          inc_amt=16, dma=True)
        E("g", lambda g: g.dma_start(out=d_row, in_=decay.unsqueeze(0)),
          inc=sMisc, inc_amt=16, dma=True)
        E("g", lambda g: g.dma_start(out=srow[0:1, 80:96], in_=percore),
          inc=sMisc, inc_amt=16, dma=True)
        E("g", lambda g: g.dma_start(out=gwslot[:, :], in_=gate_t),
          inc=sMisc, inc_amt=16, dma=True)
        E("g", lambda g: g.dma_start(out=iotarow[:, :], in_=iota_c),
          inc=sMisc, inc_amt=16, dma=True)
        E("g", lambda g: g.dma_start(out=onesbf[:, :], in_=ones_c),
          inc=sMisc, inc_amt=16, dma=True)
        misc_done = (sMisc, sMisc.n)

        E("v", lambda v: v.memset(srow[0:1, 96:97], EPS_LN), waits=[misc_done])
        E("v", lambda v: v.memset(srow[0:1, 97:98], 1e-38))
        E("v", lambda v: v.memset(sin[:, :], 0.0), inc=sV)
        sin_zeroed = (sV, sV.n)

        E("s", lambda s: s.activation(d_row, d_row, AF.Sigmoid),
          waits=[misc_done, sin_zeroed])
        E("s", lambda s: s.activation(lnd_row, d_row, AF.Ln, bias=eps_38_c),
          inc=sA)
        dprep = (sA, sA.n)

        E("t", lambda t: t.matmul(pb[7][:, 0:16], onesrow, d_row, start=True,
                                  stop=True), waits=[dprep], inc=sT)
        E("s", lambda s: s.copy(dcol[:, :], pb[7][:, 0:16]),
          waits=[(sT, sT.n)], inc=sA)
        dcol_done = (sA, sA.n)
        bank_free[7] = dcol_done

        xtm_cons = [0, 0]
        for c in range(8):
            par = c % 2
            waits = [(sT, xtm_cons[par])] if c >= 2 else []
            E("y", lambda y, c=c, par=par: y.dma_start(
                out=xtm[par][:, :],
                in_=x_in.rearrange("(c p) f -> c p f", p=128)[c]),
              waits=waits, inc=sXin[par], inc_amt=16)
            xin_wait = (sXin[par], sXin[par].n)
            for b4 in range(4):
                bk = b4
                waits_t = [xin_wait] + bank_wait(bk)
                for i4 in range(4):
                    kc = b4 * 4 + i4
                    E("t", lambda t, par=par, kc=kc, bk=bk, i4=i4:
                      t.transpose(pb[bk][:, i4 * 128:(i4 + 1) * 128],
                                  xtm[par][:, kc * 128:(kc + 1) * 128],
                                  identb[:, :]),
                      waits=(waits_t if i4 == 0 else []),
                      inc=(sT if i4 == 3 else None))
                bank_ready[bk] = (sT, sT.n)
                dst = xres.rearrange("p (kc t) -> p kc t", t=TC)[
                    :, b4 * 4:(b4 + 1) * 4, c * 128:(c + 1) * 128]
                src = pb[bk][:, 0:512].rearrange("p (i j) -> p i j", j=128)
                E("s", lambda s, dst=dst, src=src: s.copy(dst, src),
                  waits=[bank_ready[bk]], inc=sA)
                bank_free[bk] = (sA, sA.n)
            xtm_cons[par] = sT.n
        xin_done = (sA, sA.n)

        for _ in range(NSLOT):
            ring_issue()

        def emit_ln(ready_wait):
            if not isinstance(ready_wait, list):
                ready_wait = [ready_wait] * KC
            stat_banks = {0: 4, 1: 5}
            sq_banks = {0: 0, 1: 1}
            sq_cons = [0, 0]
            sq_keys = ("s", "v", "g")
            for kc in range(KC):
                sq = sqb[kc % 2]
                sq_wait = [ready_wait[kc]]
                if kc >= 2:
                    sq_wait.append((sT, sq_cons[kc % 2]))
                skey = sq_keys[kc % 3]
                if skey == "s":
                    E("s", lambda s, kc=kc, sq=sq: s.square(
                        sq[:, :], xres[:, kc * TC:(kc + 1) * TC]),
                      waits=sq_wait)
                else:
                    E(skey, lambda e, kc=kc, sq=sq: e.tensor_mul(
                        sq[:, :], xres[:, kc * TC:(kc + 1) * TC],
                        xres[:, kc * TC:(kc + 1) * TC]),
                      waits=sq_wait)
                es = sch.esem[skey]
                sq_ready = (es, es.n)
                for th in range(2):
                    bkx = stat_banks[th]
                    w = [ready_wait[kc]] + (bank_wait(bkx) if kc == 0 else [])
                    E("t", lambda t, kc=kc, th=th, bkx=bkx: t.matmul(
                        pb[bkx][0:1, :], onescol,
                        xres[:, kc * TC + th * 512: kc * TC + (th + 1) * 512],
                        start=(kc == 0), stop=(kc == KC - 1)),
                      waits=w, inc=(sT if kc == KC - 1 else None))
                    bkq = sq_banks[th]
                    w = [sq_ready] + (bank_wait(bkq) if kc == 0 else [])
                    E("t", lambda t, th=th, bkq=bkq, sq=sq: t.matmul(
                        pb[bkq][0:1, :], onescol,
                        sq[:, th * 512:(th + 1) * 512],
                        start=(kc == 0), stop=(kc == KC - 1)),
                      waits=w, inc=(sT if (kc == KC - 1 or th == 1) else None))
                sq_cons[kc % 2] = sT.n
            stats_done = (sT, sT.n)
            E("v", lambda v: v.tensor_copy(mrow[:, 0:512], pb[4][0:1, :]),
              waits=[stats_done])
            E("v", lambda v: v.tensor_copy(mrow[:, 512:1024], pb[5][0:1, :]))
            E("v", lambda v: v.tensor_copy(varrow[:, 0:512], pb[0][0:1, :]))
            E("v", lambda v: v.tensor_copy(varrow[:, 512:1024], pb[1][0:1, :]))
            E("v", lambda v: v.tensor_scalar_mul(mrow, mrow, 1.0 / DM))
            E("v", lambda v: v.tensor_scalar_mul(varrow, varrow, 1.0 / DM))
            E("v", lambda v: v.tensor_mul(arow, mrow, mrow))
            E("v", lambda v: v.tensor_sub(varrow, varrow, arow), inc=sV)
            var_done = (sV, sV.n)
            for i in (0, 1, 4, 5):
                bank_free[i] = var_done
            E("s", lambda s: s.activation(arow, varrow, AF.Ln, bias=eps_ln_c),
              waits=[var_done])
            E("s", lambda s: s.activation(arow, arow, AF.Exp, scale=-0.5),
              inc=sA)
            rstd_done = (sA, sA.n)
            E("v", lambda v: v.tensor_mul(brow, mrow, arow), waits=[rstd_done],
              inc=sV)
            rows_done = (sV, sV.n)
            for bk, row, dst in ((2, arow, abc), (3, brow, bbc)):
                for th in range(2):
                    bki = bk + 4 * th
                    E("t", lambda t, bki=bki, row=row, th=th: t.matmul(
                        pb[bki][:, :], onesrow,
                        row[0:1, th * 512:(th + 1) * 512],
                        start=True, stop=True),
                      waits=[rows_done] + bank_wait(bki), inc=sT)
                    bank_ready[bki] = (sT, sT.n)
                    E("s", lambda s, bki=bki, dst=dst, th=th: s.copy(
                        dst[:, th * 512:(th + 1) * 512], pb[bki][:, :]),
                      waits=[bank_ready[bki]], inc=sA)
                    bank_free[bki] = (sA, sA.n)
            bcast_done = (sA, sA.n)
            marks = []
            for kc in range(KC):
                key = "g" if kc % 3 == 2 else "v"
                sem = sP if key == "g" else sV
                scratch = sqb[1] if key == "g" else sqb[0]
                E(key, lambda e, kc=kc, scratch=scratch: e.tensor_mul(
                    scratch[:, :], xres[:, kc * TC:(kc + 1) * TC], abc[:, :]),
                  waits=[bcast_done])
                E(key, lambda e, kc=kc, scratch=scratch: e.tensor_sub(
                    xhat[:, kc * TC:(kc + 1) * TC], scratch[:, :], bbc[:, :]),
                  inc=sem)
                marks.append((sem, sem.n))
            return marks, (sV, sV.n), (sP, sP.n)

        ln1_marks, ln1_v, ln1_p = emit_ln(xin_done)
        if DEBUG:
            E("y", lambda y: y.dma_start(out=dbg_xhat1, in_=xhat[:, :]),
              waits=[ln1_v, ln1_p], inc=sDbg, inc_amt=16)

        qb = [S0, S1]
        kb = [S2, S3]
        vb = [S4, S5]
        ykvb = [S6, sqb[1]]
        qkv_pairs = [(0, 1), (2, 3), (4, 5)]
        head_buf_free = {0: None, 1: None}
        sbar_marks = [None] * KC

        for half in range(2):
            for twoi in range(2):
                E("v", lambda v, half=half, twoi=twoi: v.tensor_copy(
                    lnd2[half].rearrange("o (i two) -> o two i", two=2)[
                        :, twoi, :],
                    lnd_row[0:1, half * 8:(half + 1) * 8]),
                  waits=([dprep] if half == 0 and twoi == 0 else []),
                  inc=(sV if half == 1 and twoi == 1 else None))
        lnd2_done = (sV, sV.n)

        def emit_gather_issue(half, sbar_mark):
            cin = cc_inA if half == 0 else cc_inB
            cout = cc_outA if half == 0 else cc_outB
            E("g", lambda g, cin=cin, half=half: g.dma_start(
                out=cin, in_=sbar[:, half * 16:(half + 1) * 16]),
              waits=[sbar_mark], inc=sMisc, inc_amt=16, dma=True)
            E("g", lambda g, cin=cin, cout=cout: g.collective_compute(
                "AllGather", OP.bypass, replica_groups=[list(range(NCORES))],
                ins=[cin.opt()], outs=[cout.opt()]),
              waits=[(sMisc, sMisc.n)], inc=sCC, inc_amt=1, dma=True)
            return (sCC, sCC.n)

        def emit_gather_load(half, cc_mark, queue="g"):
            cout = cc_outA if half == 0 else cc_outB
            E(queue, lambda g, cout=cout, half=half: g.dma_start(
                out=gath[:, half * 128:(half + 1) * 128].rearrange(
                    "p (r c) -> p r c", c=16),
                in_=cout.rearrange("(r p) c -> p r c", p=128)),
              waits=[cc_mark], inc=sMisc, inc_amt=16, dma=True)
            return (sMisc, sMisc.n)

        def emit_sin(half, gw_done):
            for r in range(NCORES):
                w0 = []
                if r == 0:
                    w0 = [lnd2_done]
                    if sin_state["pe"] is not None:
                        w0.append(sin_state["pe"])
                E("s", lambda s, half=half, r=r: s.activation(
                    coefall[0:1, r * 16:(r + 1) * 16], lnd2[half], AF.Exp,
                    scale=gv_row[0:1, r:r + 1]),
                  waits=w0)
            for r in range(NCORES):
                E("v", lambda v, r=r: v.tensor_scalar_mul(
                    coefall[0:1, r * 16:(r + 1) * 16],
                    coefall[0:1, r * 16:(r + 1) * 16],
                    sel_row[0:1, r:r + 1]),
                  waits=([(sA, sA.n)] if r == 0 else []),
                  inc=(sV if r == NCORES - 1 else None))
            cf = (sV, sV.n)
            E("t", lambda t: t.matmul(pb[6][:, 0:128], onesrow,
                                      coefall[0:1, :], start=True, stop=True),
              waits=[cf] + bank_wait(6), inc=sT)
            bank_ready[6] = (sT, sT.n)
            sin_state["pe"] = (sT, sT.n)
            E("v", lambda v, half=half: v.tensor_mul(
                scr8[:, 0:128],
                gath[:, half * 128:(half + 1) * 128], pb[6][:, 0:128]),
              waits=[bank_ready[6], gw_done, sin_zeroed], inc=sV)
            bank_free[6] = (sV, sV.n)
            E("v", lambda v: v.tensor_add(scr8[:, 0:64], scr8[:, 0:64],
                                          scr8[:, 64:128]))
            E("v", lambda v: v.tensor_add(scr8[:, 0:32], scr8[:, 0:32],
                                          scr8[:, 32:64]))
            E("v", lambda v, half=half: v.tensor_add(
                sin[:, half * 16:(half + 1) * 16],
                scr8[:, 0:16], scr8[:, 16:32]), inc=sV)
            return (sV, sV.n)

        gath_done = [None, None]
        sin_done2 = [None, None]
        sin_state = {"pe": None}

        for h in range(KC):
            par = h % 2
            col_psum = {}
            for ci, role in enumerate(("q", "k", "v")):
                slot, wl_wait = ring_consume()
                pr = qkv_pairs[ci]
                waits = [wl_wait]
                waits += bank_wait(pr[0]) + bank_wait(pr[1])
                for kc in range(KC):
                    kw = list(waits) if kc == 0 else []
                    if h < 2:
                        kw.append(ln1_marks[kc])
                    for th in range(2):
                        E("t", lambda t, kc=kc, th=th, pr=pr, slot=slot:
                          t.matmul(pb[pr[th]][:, :], wtile(slot, kc),
                                   xhat[:, kc * TC + th * 512:
                                        kc * TC + (th + 1) * 512],
                                   start=(kc == 0), stop=(kc == KC - 1)),
                          waits=(kw if th == 0 else []),
                          inc=(sT if kc == KC - 1 and th == 1 else None))
                ring_done(slot)
                col_psum[role] = (pr, (sT, sT.n))
            bufw = [head_buf_free[par]] if head_buf_free[par] else []
            phi_done = {}
            for role, buf in (("q", qb[par]), ("k", kb[par])):
                pr, rdy = col_psum[role]
                exm = []
                for th in range(2):
                    sl = buf[:, th * 512:(th + 1) * 512]
                    E("v", lambda v, pr=pr, th=th, sl=sl:
                      v.tensor_single_scalar(sl, pb[pr[th]][:, :], 0.0, OP.min),
                      waits=([rdy] + bufw if th == 0 else []), inc=sV)
                    mn = (sV, sV.n)
                    E("s", lambda s, sl=sl: s.activation(sl, sl, AF.Exp),
                      waits=[mn], inc=sA)
                    exm.append((sA, sA.n))
                for th in range(2):
                    sl = buf[:, th * 512:(th + 1) * 512]
                    E("v", lambda v, pr=pr, th=th, sl=sl:
                      v.scalar_tensor_tensor(sl, pb[pr[th]][:, :], 0.0, sl,
                                             OP.max, OP.add),
                      waits=[exm[th]], inc=sV)
                phi_done[role] = (sV, sV.n)
                bank_free[pr[0]] = phi_done[role]
                bank_free[pr[1]] = phi_done[role]
            pr, rdy = col_psum["v"]
            for th in range(2):
                E("s", lambda s, pr=pr, th=th, par=par: s.copy(
                    vb[par][:, th * 512:(th + 1) * 512], pb[pr[th]][:, :]),
                  waits=([rdy] + bufw if th == 0 else []), inc=sA)
            v_done = (sA, sA.n)
            bank_free[pr[0]] = v_done
            bank_free[pr[1]] = v_done
            E("v", lambda v, par=par: v.tensor_mul(
                vb[par][:, :], kb[par][:, :], vb[par][:, :]),
              waits=[v_done], inc=sV)
            kv_done = (sV, sV.n)
            scan1_w = [dcol_done] + bufw
            if h < 2:
                scan1_w.append(ln1_p)
            E("v", lambda v, h=h, par=par: v.tensor_tensor_scan(
                ykvb[par][:, :], dcol[:, h:h + 1].broadcast_to((128, TC)),
                vb[par][:, :], 0.0, OP.mult, OP.add), waits=scan1_w)
            E("v", lambda v, h=h, par=par: v.tensor_copy(
                sbar[:, 2 * h:2 * h + 1], ykvb[par][:, 1023:1024]))
            E("v", lambda v, h=h, par=par: v.tensor_tensor_scan(
                vb[par][:, :], dcol[:, h:h + 1].broadcast_to((128, TC)),
                kb[par][:, :], 0.0, OP.mult, OP.add))
            E("v", lambda v, h=h, par=par: v.tensor_copy(
                sbar[:, 2 * h + 1:2 * h + 2], vb[par][:, 1023:1024]), inc=sV)
            sbar_marks[h] = (sV, sV.n)
            E("y", lambda y, h=h, par=par: y.dma_start(
                out=sp_q.rearrange("(hh p) t -> hh p t", p=128)[h],
                in_=qb[par][:, :]), waits=[phi_done["q"]], inc=sSp, inc_amt=16)
            E("y", lambda y, h=h, par=par: y.dma_start(
                out=sp_ykv.rearrange("(hh p) t -> hh p t", p=128)[h],
                in_=ykvb[par][:, :]), waits=[sbar_marks[h]], inc=sSp,
              inc_amt=16)
            E("y", lambda y, h=h, par=par: y.dma_start(
                out=sp_yk.rearrange("(hh p) t -> hh p t", p=128)[h],
                in_=vb[par][:, :]), waits=[], inc=sSp, inc_amt=16)
            head_buf_free[par] = (sSp, sSp.n)
            if h == 7:
                ccA = emit_gather_issue(0, sbar_marks[7])
            if h == 13:
                gath_done[0] = emit_gather_load(0, ccA)
                sin_done2[0] = emit_sin(0, gath_done[0])
        spill_all = (sSp, sSp.n)
        ccB = emit_gather_issue(1, sbar_marks[15])

        p3sets = [(S0, S1, S2), (S3, S4, S5), (S6, S7, S8)]
        p3_slot_free = {0: None, 1: None, 2: None}
        dd_banks = [(0, 1), (2, 3)]
        bc_banks = [(4, 5), (6, 7)]
        p3_state = {"dp_cons": None}
        corr_marks = [None] * KC

        def p3_stage1(h):
            si = h % 3
            bq, by1, by2 = p3sets[si]
            waits_in = [spill_all]
            if p3_slot_free[si]:
                waits_in.append(p3_slot_free[si])
            for src, dst in ((sp_q, bq), (sp_ykv, by1), (sp_yk, by2)):
                E("y", lambda y, src=src, dst=dst, h=h: y.dma_start(
                    out=dst[:, :],
                    in_=src.rearrange("(hh p) t -> hh p t", p=128)[h]),
                  waits=waits_in, inc=sIn[si], inc_amt=16)
                waits_in = []
            in_done = (sIn[si], sIn[si].n)
            E("s", lambda s, h=h: s.activation(
                dprow[0:1, :], iotarow[0:1, :], AF.Exp,
                scale=lnd_row[0:1, h:h + 1]),
              waits=([misc_done, dprep] if p3_state["dp_cons"] is None
                     else [p3_state["dp_cons"]]))
            dp_done = (sA, sA.n)
            dbk = dd_banks[h % 2]
            for th in range(2):
                E("t", lambda t, th=th, dbk=dbk: t.matmul(
                    pb[dbk[th]][:, :], onesbf[0:1, :],
                    dprow[0:1, th * 512:(th + 1) * 512],
                    start=True, stop=True),
                  waits=([dp_done] + bank_wait(dbk[0]) + bank_wait(dbk[1])
                         if th == 0 else []),
                  inc=(sT if th == 1 else None))
            dpmm = (sT, sT.n)
            p3_state["dp_cons"] = dpmm
            sin_w = sin_done2[0] if h < 8 else sin_done2[1]
            first = [dpmm, in_done, sin_w]
            for ci, ybuf in ((0, by1), (1, by2)):
                for th in range(2):
                    E("v", lambda v, ci=ci, ybuf=ybuf, th=th, h=h, dbk=dbk:
                      v.scalar_tensor_tensor(
                        ybuf[:, th * 512:(th + 1) * 512], pb[dbk[th]][:, :],
                        sin[:, 2 * h + ci:2 * h + ci + 1],
                        ybuf[:, th * 512:(th + 1) * 512], OP.mult, OP.add),
                      waits=first,
                      inc=(sV if ci == 1 and th == 1 else None))
                    first = []
            corr_marks[h] = (sV, sV.n)

        def p3_stage2(h):
            si = h % 3
            bq, by1, by2 = p3sets[si]
            dbk = dd_banks[h % 2]
            bcb = bc_banks[h % 2]
            dnr = denrows[h % 2]
            rdn = rdens[h % 2]
            prodsl = xhat[:, h * TC:(h + 1) * TC]
            E("g", lambda g, bq=bq, by2=by2, prodsl=prodsl: g.tensor_mul(
                prodsl, bq[:, :], by2[:, :]),
              waits=[corr_marks[h]], inc=sP)
            prod_done = (sP, sP.n)
            for th in range(2):
                E("t", lambda t, th=th, dbk=dbk, prodsl=prodsl: t.matmul(
                    pb[dbk[th]][0:1, :], onesbf[:, 0:1],
                    prodsl[:, th * 512:(th + 1) * 512],
                    start=True, stop=True),
                  waits=([prod_done] if th == 0 else []),
                  inc=(sT if th == 1 else None))
            den_mm = (sT, sT.n)
            for th in range(2):
                E("v", lambda v, th=th, dbk=dbk, dnr=dnr:
                  v.tensor_single_scalar(
                    dnr[0:1, th * 512:(th + 1) * 512], pb[dbk[th]][0:1, :],
                    EPS_DEN, OP.max),
                  waits=([den_mm] if th == 0 else []),
                  inc=(sV if th == 1 else None))
            den_row_done = (sV, sV.n)
            bank_free[dbk[0]] = den_row_done
            bank_free[dbk[1]] = den_row_done
            E("s", lambda s, dnr=dnr: s.activation(dnr, dnr, AF.Ln),
              waits=[den_row_done])
            E("s", lambda s, rdn=rdn, dnr=dnr: s.activation(
                rdn, dnr, AF.Exp, scale=-1.0), inc=sA)
            rden_done = (sA, sA.n)
            for th in range(2):
                E("t", lambda t, th=th, bcb=bcb, rdn=rdn: t.matmul(
                    pb[bcb[th]][:, :], onesbf[0:1, :],
                    rdn[0:1, th * 512:(th + 1) * 512], start=True, stop=True),
                  waits=([rden_done] + bank_wait(bcb[0]) + bank_wait(bcb[1])
                         if th == 0 else []),
                  inc=(sT if th == 1 else None))
            bc_done = (sT, sT.n)
            E("g", lambda g, bq=bq, by1=by1: g.tensor_mul(
                bq[:, :], bq[:, :], by1[:, :]),
              waits=[corr_marks[h]], inc=sP)
            qm_done = (sP, sP.n)
            for th in range(2):
                E("v", lambda v, th=th, h=h, bq=bq, bcb=bcb: v.tensor_mul(
                    xhat[:, h * TC + th * 512: h * TC + (th + 1) * 512],
                    bq[:, th * 512:(th + 1) * 512], pb[bcb[th]][:, :]),
                  waits=([bc_done, qm_done] if th == 0 else []),
                  inc=(sV if th == 1 else None))
            outpre_done = (sV, sV.n)
            bank_free[bcb[0]] = outpre_done
            bank_free[bcb[1]] = outpre_done
            p3_slot_free[si] = outpre_done

        for h in range(KC):
            if h == 7:
                gath_done[1] = emit_gather_load(1, ccB, queue="y")
            if h == 8:
                p3_stage2(6)
                sin_done2[1] = emit_sin(1, gath_done[1])
            elif h >= 2:
                p3_stage2(h - 2)
            p3_stage1(h)
        p3_stage2(KC - 2)
        p3_stage2(KC - 1)
        outpre_all = (sV, sV.n)
        if DEBUG:
            E("y", lambda y: y.dma_start(out=dbg_outpre, in_=xhat[:, :]),
              waits=[outpre_all], inc=sDbg, inc_amt=16)
            E("y", lambda y: y.dma_start(out=dbg_small[:, 0:32],
                                         in_=sbar[:, :]),
              waits=[sbar_marks[15]], inc=sDbg, inc_amt=16)
            E("y", lambda y: y.dma_start(out=dbg_small[:, 32:64],
                                         in_=sin[:, :]),
              waits=[sin_done2[1]], inc=sDbg, inc_amt=16)
            E("y", lambda y: y.dma_start(out=dbg_small[:, 64:320],
                                         in_=gath[:, :]),
              waits=[gath_done[1]], inc=sDbg, inc_amt=16)
            E("y", lambda y: y.dma_start(out=dbg_small[:, 320:336],
                                         in_=dcol[:, :]),
              waits=[dcol_done], inc=sDbg, inc_amt=16)
            E("y", lambda y: y.dma_start(out=dbg_small[0:1, 336:464],
                                         in_=srow[0:1, 0:128]),
              waits=[sin_done2[1]], inc=sDbg, inc_amt=16)

        op_pairs = [(0, 1), (2, 3)]
        xres_marks = []
        for ncol in range(16):
            slot, wl_wait = ring_consume()
            pr = op_pairs[ncol % 2]
            waits = [wl_wait, outpre_all] + bank_wait(pr[0]) + bank_wait(pr[1])
            for kc in range(KC):
                for th in range(2):
                    E("t", lambda t, kc=kc, th=th, pr=pr, slot=slot: t.matmul(
                        pb[pr[th]][:, :], wtile(slot, kc),
                        xhat[:, kc * TC + th * 512: kc * TC + (th + 1) * 512],
                        start=(kc == 0), stop=(kc == KC - 1)),
                      waits=(waits if kc == 0 and th == 0 else []),
                      inc=(sT if kc == KC - 1 and th == 1 else None))
            ring_done(slot)
            rdy = (sT, sT.n)
            for th in range(2):
                sl = xres[:, ncol * TC + th * 512: ncol * TC + (th + 1) * 512]
                E("v", lambda v, sl=sl, pr=pr, th=th: v.tensor_add(
                    sl, sl, pb[pr[th]][:, :]),
                  waits=([rdy] if th == 0 else []),
                  inc=(sV if th == 1 else None))
            ev = (sV, sV.n)
            bank_free[pr[0]] = ev
            bank_free[pr[1]] = ev
            xres_marks.append(ev)
        attn_done = (sV, sV.n)
        if DEBUG:
            E("y", lambda y: y.dma_start(out=dbg_xres, in_=xres[:, :]),
              waits=[attn_done], inc=sDbg, inc_amt=16)

        ln2_marks, ln2_v, ln2_p = emit_ln(xres_marks)

        mx = scr8[:, 0:1]
        mxn = scr8[:, 1:2]
        sume = scr8[:, 2:3]
        rsum = scr8[:, 3:4]
        escr = scr8[:, 8:16]
        cnt = scr8[:, 16:24]
        tmp8 = scr8[:, 24:32]
        comb_marks = [None] * 8
        for tc in range(8):
            bk = 4 + tc % 4
            scomb = scr8[:, 64 + tc * 8: 72 + tc * 8]
            waits = [misc_done] + bank_wait(bk)
            for kc in range(KC):
                w = []
                if tc < 2:
                    w.append(ln2_marks[kc])
                if kc == 0:
                    w = w + waits
                E("t", lambda t, kc=kc, tc=tc, bk=bk: t.matmul(
                    pb[bk][:, 0:8],
                    xhat[:, kc * TC + tc * 128: kc * TC + (tc + 1) * 128],
                    gwslot[:, kc * 8:(kc + 1) * 8],
                    start=(kc == 0), stop=(kc == KC - 1)),
                  waits=w, inc=(sT if kc == KC - 1 else None))
            lg = (sT, sT.n)
            E("v", lambda v, bk=bk: v.reduce_max(mx, pb[bk][:, 0:8], axis=AX),
              waits=[lg])
            E("v", lambda v: v.tensor_scalar_mul(mxn, mx, -1.0), inc=sV)
            E("s", lambda s, bk=bk: s.activation(escr, pb[bk][:, 0:8], AF.Exp,
                                                 bias=mxn),
              waits=[(sV, sV.n)], inc=sA)
            ex_done = (sA, sA.n)
            bank_free[bk] = ex_done
            E("v", lambda v: v.reduce_sum(sume, escr, axis=AX), waits=[ex_done])
            E("v", lambda v: v.reciprocal(rsum, sume))
            E("v", lambda v: v.tensor_scalar_mul(escr, escr, rsum))
            E("v", lambda v: v.memset(cnt, 0.0))
            for j in range(8):
                E("v", lambda v, j=j: v.tensor_single_scalar(
                    tmp8, escr, escr[:, j:j + 1], OP.is_lt))
                E("v", lambda v: v.tensor_add(cnt, cnt, tmp8))
            E("v", lambda v: v.tensor_single_scalar(tmp8, cnt, 1.5, OP.is_lt))
            E("v", lambda v, scomb=scomb: v.tensor_mul(scomb, escr, tmp8),
              inc=sV)
            comb_marks[tc] = (sV, sV.n)

        def emit_combT():
            for tc in range(8):
                scomb = scr8[:, 64 + tc * 8: 72 + tc * 8]
                bkt = 6 if tc % 2 == 0 else 7
                E("t", lambda t, bkt=bkt, scomb=scomb: t.transpose(
                    pb[bkt][0:8, 0:128], scomb, identb[:, :]),
                  waits=[comb_marks[tc]] + bank_wait(bkt), inc=sT)
                tr_done = (sT, sT.n)
                E("s", lambda s, tc=tc, bkt=bkt: s.copy(
                    combT[:, tc * 128:(tc + 1) * 128], pb[bkt][0:8, 0:128]),
                  waits=[tr_done], inc=sA)
                bank_free[bkt] = (sA, sA.n)
            return (sA, sA.n)

        gu_banks = (0, 1, 2, 3)
        down_pairs = [(4, 5), (6, 7)]
        prev_gmark = None
        combT_done = None
        for ji, job in enumerate(moe_jobs):
            if ji == 1:
                combT_done = emit_combT()
            combi = job["comb"]
            cslot = ji % 2
            if combi is not None:
                for th in range(2):
                    bk = gu_banks[2 + th]
                    E("t", lambda t, combi=combi, th=th, bk=bk: t.matmul(
                        pb[bk][:, :], seleb[:, combi * 128:(combi + 1) * 128],
                        combT[:, th * 512:(th + 1) * 512],
                        start=True, stop=True),
                      waits=[combT_done] + bank_wait(bk), inc=sT)
                    bank_ready[bk] = (sT, sT.n)
                    E("s", lambda s, th=th, bk=bk, cslot=cslot: s.copy(
                        combbc[:, cslot * TC + th * 512:
                               cslot * TC + (th + 1) * 512],
                        pb[bk][:, :]),
                      waits=[bank_ready[bk]], inc=sA)
                    bank_free[bk] = (sA, sA.n)
            combbc_done = (sA, sA.n)
            g_all = None
            ev_prev = None
            for hc in range(HC_E):
                gub = (0, 1, 2, 3) if hc % 2 == 0 else (4, 5, 6, 7)
                sg, wg_wait = ring_consume()
                su, wu_wait = ring_consume()
                waits0 = [wg_wait, wu_wait]
                if ji > 0:
                    waits0.append(ln2_v)
                    waits0.append(ln2_p)
                for b in gub:
                    waits0 += bank_wait(b)
                for wi, (slot, b0) in enumerate(((sg, 0), (su, 2))):
                    for th in range(2):
                        bk = gub[b0 + th]
                        for kc in range(KC):
                            kw = (list(waits0) if wi == 0 and th == 0
                                  and kc == 0 else [])
                            if ji == 0 and hc == 0 and wi == 0 and th == 0:
                                kw.append(ln2_marks[kc])
                            E("t", lambda t, kc=kc, bk=bk, th=th, slot=slot:
                              t.matmul(pb[bk][:, :], wtile(slot, kc),
                                       xhat[:, kc * TC + th * 512:
                                            kc * TC + (th + 1) * 512],
                                       start=(kc == 0), stop=(kc == KC - 1)),
                              waits=kw,
                              inc=(sT if kc == KC - 1 else None))
                ring_done(sg)
                ring_done(su)
                gu_rdy = (sT, sT.n)
                gw = [gu_rdy]
                if ev_prev is not None:
                    gw.append(ev_prev)
                if prev_gmark is not None:
                    gw.append(prev_gmark)
                if combi is not None:
                    gw.append(combbc_done)
                for th in range(2):
                    E("s", lambda s, th=th, gub=gub: s.activation(
                        gsil[:, th * 512:(th + 1) * 512],
                        pb[gub[th]][:, :], AF.Silu),
                      waits=(gw if th == 0 else []), inc=sA)
                    sil = (sA, sA.n)
                    gdst = gar[:, hc * TC + th * 512: hc * TC + (th + 1) * 512]
                    if combi is not None:
                        E("v", lambda v, th=th, gub=gub: v.scalar_tensor_tensor(
                            gtmp[:, th * 512:(th + 1) * 512],
                            pb[gub[2 + th]][:, :], 0.0,
                            gsil[:, th * 512:(th + 1) * 512],
                            OP.add, OP.mult),
                          waits=[sil])
                        E("v", lambda v, th=th, gdst=gdst, cslot=cslot:
                          v.tensor_mul(
                            gdst, gtmp[:, th * 512:(th + 1) * 512],
                            combbc[:, cslot * TC + th * 512:
                                   cslot * TC + (th + 1) * 512]),
                          inc=sV)
                    else:
                        E("v", lambda v, th=th, gdst=gdst, gub=gub:
                          v.scalar_tensor_tensor(
                            gdst, pb[gub[2 + th]][:, :], 0.0,
                            gsil[:, th * 512:(th + 1) * 512],
                            OP.add, OP.mult),
                          waits=[sil], inc=sV)
                ev = (sV, sV.n)
                ev_prev = ev
                g_all = ev
                for b in gub:
                    bank_free[b] = ev
            for ncol in range(16):
                slot, wd_wait = ring_consume()
                pr = down_pairs[ncol % 2]
                waits = [wd_wait, g_all] + bank_wait(pr[0]) + bank_wait(pr[1])
                for kc in range(HC_E):
                    for th in range(2):
                        E("t", lambda t, kc=kc, th=th, pr=pr, slot=slot:
                          t.matmul(pb[pr[th]][:, :], wtile(slot, kc),
                                   gar[:, kc * TC + th * 512:
                                       kc * TC + (th + 1) * 512],
                                   start=(kc == 0), stop=(kc == HC_E - 1)),
                          waits=(waits if kc == 0 and th == 0 else []),
                          inc=(sT if kc == HC_E - 1 and th == 1 else None))
                ring_done(slot)
                rdy = (sT, sT.n)
                for th in range(2):
                    sl = xres[:, ncol * TC + th * 512:
                              ncol * TC + (th + 1) * 512]
                    E("v", lambda v, sl=sl, pr=pr, th=th: v.tensor_add(
                        sl, sl, pb[pr[th]][:, :]),
                      waits=([rdy] if th == 0 else []),
                      inc=(sV if th == 1 else None))
                ev = (sV, sV.n)
                bank_free[pr[0]] = ev
                bank_free[pr[1]] = ev
            prev_gmark = (sT, sT.n)
        moe_done = (sV, sV.n)

        for c in range(8):
            par = c % 2
            for b4 in range(4):
                bk = b4
                waits_t = [moe_done] + bank_wait(bk)
                if c >= 2 and b4 == 0:
                    waits_t.append((sOd[par], 16 * (c // 2)))
                for i4 in range(4):
                    kc = b4 * 4 + i4
                    E("t", lambda t, kc=kc, c=c, bk=bk, i4=i4: t.transpose(
                        pb[bk][:, i4 * 128:(i4 + 1) * 128],
                        xres[:, kc * TC + c * 128: kc * TC + (c + 1) * 128],
                        identb[:, :]),
                      waits=(waits_t if i4 == 0 else []),
                      inc=(sT if i4 == 3 else None))
                bank_ready[bk] = (sT, sT.n)
                dst = xtm[par].rearrange("p (kk j) -> p kk j", j=128)[
                    :, b4 * 4:(b4 + 1) * 4, :]
                src = pb[bk][:, 0:512].rearrange("p (i j) -> p i j", j=128)
                E("s", lambda s, dst=dst, src=src: s.copy(dst, src),
                  waits=[bank_ready[bk]], inc=sA)
                bank_free[bk] = (sA, sA.n)
            ev = (sA, sA.n)
            E("y", lambda y, c=c, par=par: y.dma_start(
                out=out_t.rearrange("(c p) f -> c p f", p=128)[c],
                in_=xtm[par][:, :]), waits=[ev], inc=sOd[par], inc_amt=16)
        final_waits = [(sOd[0], sOd[0].n), (sOd[1], sOd[1].n)]
        if DEBUG:
            final_waits.append((sDbg, sDbg.n))
        E("y", lambda y: None, waits=final_waits)

        assert ring["issued"] == len(wplan) and ring["consumed"] == len(wplan)

        block.tensor(sch.run_key("t"))
        block.vector(sch.run_key("v"))
        block.scalar(sch.run_key("s"))
        block.gpsimd(sch.run_key("g"))
        block.sync(sch.run_key("y"))

    return nc


_PROG_CACHE = {}


def _get_program():
    if "nc" not in _PROG_CACHE:
        _PROG_CACHE["nc"] = build_program()
    return _PROG_CACHE["nc"]


def tile_km(W, nkc, ncols):
    return np.ascontiguousarray(
        W.reshape(nkc, 128, ncols, 128).transpose(2, 1, 0, 3)
        .reshape(ncols, 128, nkc * 128))


def make_in_maps(L0_attn_x, L0_attn_w_qkv, L0_attn_w_out, L0_attn_decay_param,
                 L0_ffn_gate_w, L0_ffn_w_gate, L0_ffn_w_up, L0_ffn_w_down,
                 L0_ffn_sw_gate, L0_ffn_sw_up, L0_ffn_sw_down):
    x = np.ascontiguousarray(np.asarray(L0_attn_x, np.float32).reshape(8192, DM))
    dec = np.ascontiguousarray(np.asarray(L0_attn_decay_param, np.float32))
    qkv_t = tile_km(np.asarray(L0_attn_w_qkv, np.float32), 16, 48)
    wout_t = tile_km(np.asarray(L0_attn_w_out, np.float32), 16, 16)
    gw = np.asarray(L0_ffn_gate_w, np.float32)
    gate_t = np.ascontiguousarray(
        gw.reshape(16, 128, 8).transpose(1, 0, 2).reshape(128, 128))
    wg = np.asarray(L0_ffn_w_gate, np.float32)
    wu = np.asarray(L0_ffn_w_up, np.float32)
    wd = np.asarray(L0_ffn_w_down, np.float32)
    wg_t = np.stack([tile_km(wg[e], 16, 11) for e in range(8)])
    wu_t = np.stack([tile_km(wu[e], 16, 11) for e in range(8)])
    wd_t = np.stack([tile_km(wd[e], 11, 16) for e in range(8)])
    swg_t = tile_km(np.asarray(L0_ffn_sw_gate, np.float32), 16, 22)
    swu_t = tile_km(np.asarray(L0_ffn_sw_up, np.float32), 16, 22)
    swd = np.asarray(L0_ffn_sw_down, np.float32)
    swd_t = np.stack([tile_km(swd[0:1408], 11, 16),
                      tile_km(swd[1408:2816], 11, 16)])

    ones_c = np.ones((128, 128), np.float32)
    ident_c = np.eye(128, dtype=np.float32)
    iota = np.arange(1, 1025, dtype=np.float32).reshape(1, 1024)
    sele = np.zeros((8, 1024), np.float32)
    for e in range(8):
        sele[e, e * 128:(e + 1) * 128] = 1.0
    sele = sele.astype(ml_dtypes.bfloat16)

    in_maps = []
    for c in range(NCORES):
        batch0 = (c // 4) * 4
        pc = np.zeros((1, 16), np.float32)
        for g, cp in enumerate(range(c - 1, batch0 - 1, -1)):
            pc[0, cp] = 1024.0 * g
            pc[0, 8 + cp] = 1.0
        in_maps.append({
            "x": x[c * TC:(c + 1) * TC],
            "qkv_t": qkv_t, "wout_t": wout_t, "gate_t": gate_t,
            "wg_t": wg_t, "wu_t": wu_t, "wd_t": wd_t,
            "swg_t": swg_t, "swu_t": swu_t, "swd_t": swd_t,
            "decay": dec, "percore": pc,
            "ones_c": ones_c, "ident_c": ident_c, "sele_c": sele,
            "iota_c": iota,
        })
    return in_maps


def kernel(**inputs):
    in_maps = make_in_maps(**inputs)
    nc = _get_program()
    res = run_bass_kernel_spmd(nc, in_maps, core_ids=list(range(NCORES)))
    out = np.concatenate([res.results[c]["out"] for c in range(NCORES)], axis=0)
    return out.reshape(2, 4096, DM).astype(np.float32)
